# version 02 (final)
# baseline (speedup 1.0000x reference)
import contextlib
import numpy as np
import ml_dtypes
import concourse.bass as bass
import concourse.mybir as mybir
from concourse.bass_utils import run_bass_kernel_spmd
from concourse.bass import IndirectOffsetOnAxis

F32 = mybir.dt.float32
BF16 = mybir.dt.bfloat16
I32 = mybir.dt.int32
AF = mybir.ActivationFunctionType
ALU = mybir.AluOpType
AX = mybir.AxisListType
ds = bass.ds
BF16NP = ml_dtypes.bfloat16

CFG = dict(
    D=1024, S=4096, B=4,
    HID=3072,
    NH1=16, DH1=64,
    NH3=8, DH3=128, NKV3=4,
    ME=16, MS=16, MH=256,
    E2=8, FF2=4096,
)
LN_EPS = 1e-5
DEQ = np.float32(1.0) / np.float32(127.0)
ENGINES = ("sync", "tensor", "vector", "scalar", "gpsimd")
_UID = [0]

def uid():
    _UID[0] += 1
    return _UID[0]
STRICT = True
DEBUG_L0 = False
DEBUG_L1 = False


class Prog:

    def __init__(self, nc, stack):
        self.nc = nc
        self.stack = stack
        self.q = {e: [] for e in ENGINES}
        self.sems = {}
        self.cnt = {}
        for s in ("d_sync", "d_gp", "d_sc", "d_ve", "t", "v", "a", "g"):
            self.sem(s)

    def sem(self, name):
        if name not in self.sems:
            self.sems[name] = self.stack.enter_context(self.nc.semaphore(name))
            self.cnt[name] = 0
        return name

    def do(self, eng, fn, waits=(), inc=None, inc_step=1, strict=None):
        strict = STRICT if strict is None else strict
        wd = {}
        if strict:
            for s, v in self.cnt.items():
                if v > 0:
                    wd[s] = v
        for s, v in waits:
            assert s in self.cnt, f"unknown sem {s}"
            assert v <= self.cnt[s], f"wait on future value of {s}: {v} > {self.cnt[s]}"
            wd[s] = max(wd.get(s, 0), v)
        wl = [(self.sems[s], v) for s, v in wd.items()]
        incspec = None
        if inc is not None:
            self.sem(inc)
            self.cnt[inc] += inc_step
            incspec = (self.sems[inc], inc_step)
        self.q[eng].append((wl, fn, incspec))
        return self.cnt[inc] if inc else None

    def bump(self, sem, step):
        self.sem(sem)
        self.cnt[sem] += step
        return self.cnt[sem]

    def dma(self, eng, out, in_, waits=(), sem=None, strict=None, **kw):
        sem = sem or ("d_" + {"sync": "sync", "gpsimd": "gp", "scalar": "sc", "vector": "ve"}[eng])
        self.sem(sem)
        return self.do(eng, lambda e: e.dma_start(out=out, in_=in_, **kw),
                       waits=waits, inc=sem, inc_step=16, strict=strict), sem

    def finalize(self, block):
        for e in ENGINES:
            items = self.q[e]
            if not items:
                continue

            def mk(items):
                def f(eng):
                    for wl, fn, incspec in items:
                        for h, v in wl:
                            eng.wait_ge(h, v)
                        r = fn(eng)
                        if incspec is not None:
                            assert r is not None, f"fn with inc returned None: {getattr(fn, '__code__', None) and fn.__code__.co_filename}:{fn.__code__.co_firstlineno}"
                            r.then_inc(*incspec)
                return f
            getattr(block, e)(mk(items))


def host_constants(cfg):
    c = {}
    c["ones128"] = np.ones((128, 1), np.float32)
    c["onesk1"] = np.ones((1, 128), np.float32)
    oh = np.zeros((128, 2), np.float32)
    oh[:64, 0] = 1.0
    oh[64:, 1] = 1.0
    c["ones64x2"] = oh
    el = np.zeros((1, 128), np.float32); el[0, :64] = 1.0
    eh = np.zeros((1, 128), np.float32); eh[0, 64:] = 1.0
    c["elow"] = el
    c["ehigh"] = eh
    c["id128"] = np.eye(128, dtype=BF16NP)
    m = np.zeros((4, 128, 512), np.float32)
    kk = np.arange(128)[:, None]
    qq = np.arange(512)[None, :]
    for r in range(4):
        m[r] = (r * 128 + kk <= qq).astype(np.float32)
    c["dmask"] = m.astype(BF16NP)
    io = np.zeros((128, 2), np.float32)
    io[:, 0] = np.arange(128)
    io[:, 1] = np.arange(128)
    c["iota128"] = io
    tb = np.zeros((1, 64), np.float32)
    tb[0, :] = 128.0 * np.arange(64)
    c["tbase"] = tb
    c["iota8bc"] = np.tile(np.arange(8, dtype=np.float32)[None, :], (128, 1))
    return c



def rope_tables(cfg, dh, key_order):
    S = cfg["S"]
    half = dh // 2
    inv = 1.0 / (10000.0 ** (np.arange(0, dh, 2, dtype=np.float64) / dh))
    fr = np.asarray(key_order, np.float64)[None, :] * inv[:, None]
    cos, sin = np.cos(fr), np.sin(fr)
    COS = np.zeros((128, S), np.float64)
    SIN = np.zeros((128, S), np.float64)
    for rpt in range(128 // dh):
        b = rpt * dh
        COS[b:b + half] = cos
        COS[b + half:b + dh] = cos
        SIN[b:b + half] = -sin
        SIN[b + half:b + dh] = sin
    P = np.zeros((128, 128), np.float32)
    for mm in range(128):
        blk = (mm // dh) * dh
        cc = mm % dh
        P[blk + ((cc + half) % dh), mm] = 1.0
    return (np.ascontiguousarray(COS.astype(np.float32)),
            np.ascontiguousarray(SIN.astype(np.float32)), P.astype(BF16NP))


def emit_ln(p, sb, ps, x_dram, xn_sb, S_tok, gamma_beta=None, x_sb_src=None, out32_sb=None):
    D = CFG["D"]
    NC = D // 128
    nQ = S_tok // 512
    xsl, sq, row = sb["lnx"], sb["lnsq"], sb["lnrow"]
    mrow, vrow = row[0:1, 0:512], row[0:1, 512:1024]
    arow, brow = row[0:1, 1024:1536], row[0:1, 1536:2048]
    for qt in range(nQ):
        if x_sb_src is None:
            for ci in range(NC):
                p.dma("sync", xsl[:, ci * 512:(ci + 1) * 512],
                      x_dram[ci * 128:(ci + 1) * 128, qt * 512:(qt + 1) * 512])
            src = lambda ci: xsl[:, ci * 512:(ci + 1) * 512]
        else:
            def cp(e, qt=qt):
                r = None
                for ci in range(NC):
                    r = e.activation(out=xsl[:, ci * 512:(ci + 1) * 512],
                                     in_=x_sb_src[:, ci * S_tok + qt * 512: ci * S_tok + (qt + 1) * 512],
                                     func=AF.Copy)
                return r
            p.do("scalar", cp, inc="a")
            src = lambda ci: xsl[:, ci * 512:(ci + 1) * 512]

        def squares(e, src=src):
            r = None
            for ci in range(NC):
                r = e.activation(out=sq[:, ci * 512:(ci + 1) * 512], in_=src(ci), func=AF.Square)
            return r
        p.do("scalar", squares, inc="a")

        ones_x = sb["ones128"]

        def stats(e, src=src, ones_x=ones_x):
            r = None
            for ci in range(NC):
                e.matmul(ps[0][0:1, 0:512], ones_x[:, 0:1], src(ci),
                         start=(ci == 0), stop=(ci == NC - 1))
                r = e.matmul(ps[1][0:1, 0:512], sb["ones128"][:, 0:1],
                             sq[:, ci * 512:(ci + 1) * 512],
                             start=(ci == 0), stop=(ci == NC - 1))
            return r
        p.do("tensor", stats, inc="t")

        def rowmath1(e):
            e.tensor_scalar(mrow, ps[0][0:1, 0:512], 1.0 / D, None, op0=ALU.mult)
            e.tensor_scalar(vrow, ps[1][0:1, 0:512], 1.0 / D, None, op0=ALU.mult)
            e.tensor_tensor(arow, mrow, mrow, op=ALU.mult)
            e.tensor_tensor(vrow, vrow, arow, op=ALU.subtract)
            return e.tensor_scalar(vrow, vrow, LN_EPS, None, op0=ALU.add)
        p.do("vector", rowmath1, inc="v")
        p.do("scalar", lambda e: e.activation(out=vrow, in_=vrow, func=AF.Sqrt), inc="a")

        def rowmath2(e):
            e.reciprocal(arow, vrow)
            return e.scalar_tensor_tensor(brow, mrow, -1.0, arow, op0=ALU.mult, op1=ALU.mult)
        p.do("vector", rowmath2, inc="v")

        def bcast(e):
            e.matmul(ps[2][:, 0:512], sb["onesk1"][0:1, :], arow, start=True, stop=True)
            return e.matmul(ps[3][:, 0:512], sb["onesk1"][0:1, :], brow, start=True, stop=True)
        p.do("tensor", bcast, inc="t")

        def norm(e, qt=qt, src=src):
            r = None
            for ci in range(NC):
                t1 = sq[:, ci * 512:(ci + 1) * 512]
                e.tensor_tensor(t1, src(ci), ps[2][:, 0:512], op=ALU.mult)
                dstc = xn_sb[:, ci * S_tok + qt * 512: ci * S_tok + (qt + 1) * 512]
                if gamma_beta is None:
                    r = e.tensor_tensor(dstc, t1, ps[3][:, 0:512], op=ALU.add)
                    if out32_sb is not None:
                        r = e.tensor_copy(
                            out32_sb[:, ci * S_tok + qt * 512: ci * S_tok + (qt + 1) * 512], dstc)
                else:
                    gsb, bsb = gamma_beta
                    e.tensor_tensor(t1, t1, ps[3][:, 0:512], op=ALU.add)
                    r = e.tensor_scalar(dstc, t1, gsb[:, ci:ci + 1], bsb[:, ci:ci + 1],
                                        op0=ALU.mult, op1=ALU.add)
            return r
        p.do("vector", norm, inc="v")


def build_l0(cfg):
    D, S = cfg["D"], cfg["S"]
    NC = D // 128
    T = S // 2
    HID = cfg["HID"]
    NHG = 3
    HGS = HID // NHG
    NHC = HGS // 128
    EW = min(512, D)
    NET = EW // 128
    NECH = max(1, D // EW)
    scale = float(np.float32(D) ** -0.5)

    nc = bass.Bass("TRN2", target_bir_lowering=False)
    stack = contextlib.ExitStack()

    xb = nc.dram_tensor("xb", [D, S], F32, kind="ExternalInput").ap()
    wqkvT = nc.dram_tensor("wqkvT", [D, 3 * D], F32, kind="ExternalInput").ap()
    wgT = nc.dram_tensor("wgT", [D, HID], I32, kind="ExternalInput").ap()
    wuT = nc.dram_tensor("wuT", [D, HID], I32, kind="ExternalInput").ap()
    wdT = nc.dram_tensor("wdT", [HID, D], I32, kind="ExternalInput").ap()
    ones128_d = nc.dram_tensor("c_ones128", [128, 1], F32, kind="ExternalInput").ap()
    onesk1_d = nc.dram_tensor("c_onesk1", [1, 128], F32, kind="ExternalInput").ap()
    xout = nc.dram_tensor("xout", [D, T], F32, kind="ExternalOutput").ap()
    kdram = nc.dram_tensor("kdram", [NC, 128, S], BF16)
    vdram = nc.dram_tensor("vdram", [S // 128, 128, D], BF16)
    qdram = nc.dram_tensor("qdram", [NC, 128, T], BF16)
    if DEBUG_L0:
        dbg_xn = nc.dram_tensor("dbg_xn", [NC, 128, S], BF16, kind="ExternalOutput").ap()
        dbg_k = nc.dram_tensor("dbg_k", [NC, 128, S], BF16, kind="ExternalOutput").ap()
        dbg_v = nc.dram_tensor("dbg_v", [S // 128, 128, D], BF16, kind="ExternalOutput").ap()
        dbg_q = nc.dram_tensor("dbg_q", [NC, 128, T], BF16, kind="ExternalOutput").ap()
        dbg_s = nc.dram_tensor("dbg_s", [NC, 128, T], BF16, kind="ExternalOutput").ap()
        dbg_g = nc.dram_tensor("dbg_g", [NC, 128, T], BF16, kind="ExternalOutput").ap()
        dbg_m = nc.dram_tensor("dbg_m", [1, 8], F32, kind="ExternalOutput").ap()

    sb = {}
    with stack:
        block = stack.enter_context(nc.Block())
        ps = [stack.enter_context(nc.psum_tensor(f"ps{i}", [128, 512], F32)) for i in range(8)]

        def sbuf(name, shape, dt):
            t = stack.enter_context(nc.sbuf_tensor(name, shape, dt))
            sb[name] = t
            return t

        sbuf("ones128", [128, 1], F32)
        sbuf("ones128b", [128, 1], BF16)
        sbuf("onesk1", [1, 128], F32)
        evac = sbuf("evac", [128, 512], BF16)
        sqs = sbuf("sqs", [128, 512], F32)
        mcell = sbuf("mcell", [1, 8], F32)
        negm = sbuf("negm", [128, 1], F32)
        TH = T // 2
        A1 = max(NC * S, 3 * NC * TH + 2048)
        ar1 = sbuf("ar1", [128, A1], BF16)
        ar1f = ar1[:, :].bitcast(F32)
        A2F = max(2 * NC * 512 + 2048,
                  NC * HGS + (NC * HGS + NHC * D + NHC * 512) // 2 + NC * 512,
                  (2 * NC * 1024 + NC * T) // 2,
                  NC * 1024 + S + T)
        ar2 = sbuf("ar2", [128, A2F], F32)
        ar2b = ar2[:, :].bitcast(BF16)
        xn = ar1[:, 0:NC * S]
        sb["lnx"] = ar2[:, 0:NC * 512]
        sb["lnsq"] = ar2[:, NC * 512:2 * NC * 512]
        sb["lnrow"] = ar2[0:1, 2 * NC * 512:2 * NC * 512 + 2048]
        nrm = ar2[0:1, NC * 1024:NC * 1024 + S]
        nrmq = ar2[0:1, NC * 1024 + S:NC * 1024 + S + T]
        wblk = ar2b[:, 0:NC * EW]
        wg_sb = ar2b[:, 0:NC * HGS]
        wu_sb = ar2b[:, NC * HGS:2 * NC * HGS]
        wd_sb = ar2b[:, 2 * NC * HGS:2 * NC * HGS + NHC * D]
        h_sb = ar2b[:, 2 * NC * HGS + NHC * D:2 * NC * HGS + NHC * D + NHC * 512]
        y_off = (2 * NC * HGS + NHC * D + NHC * 512) // 2
        y_sb = ar2[:, y_off:y_off + NC * 512]
        kb_sb = ar2b[:, 0:NC * 1024]
        vb_sb = ar2b[:, NC * 1024:NC * 1024 + 8 * D]
        sacc = ar1f[:, 0:NC * TH]
        qh_sb = ar1[:, 2 * NC * TH:3 * NC * TH]
        pt_sb = ar1[:, 3 * NC * TH:3 * NC * TH + 2048]

        p = Prog(nc, stack)
        p.dma("sync", sb["ones128"][:, :], ones128_d)
        p.dma("gpsimd", sb["ones128b"][:, :], ones128_d)
        p.dma("sync", sb["onesk1"][:, :], onesk1_d)

        emit_ln(p, sb, ps, xb, xn, S)

        p.do("vector", lambda e: e.memset(nrm[:, :], 0.0), inc="v")
        p.do("vector", lambda e: e.memset(nrmq[:, :], 0.0), inc="v")

        def proj_block(kind, e0, out_dram, tok_n=None, nrm_row=None):
            for ec in range(NECH):
                for ci in range(NC):
                    p.dma("gpsimd", wblk[:, ci * EW:(ci + 1) * EW],
                          wqkvT[ci * 128:(ci + 1) * 128, e0 + ec * EW: e0 + (ec + 1) * EW])
                if kind == "fm":
                    for et in range(NET):
                        for qt in range(tok_n // 512):
                            def mams(e, ec=ec, et=et, qt=qt):
                                r = None
                                for ci in range(NC):
                                    r = e.matmul(
                                        ps[4][:, 0:512],
                                        wblk[:, ci * EW + et * 128: ci * EW + (et + 1) * 128],
                                        xn[:, ci * S + qt * 512:ci * S + (qt + 1) * 512],
                                        start=(ci == 0), stop=(ci == NC - 1))
                                return r
                            p.do("tensor", mams, inc="t")
                            p.do("scalar", lambda e: e.activation(
                                out=evac[:, 0:512] if EW >= 512 else evac[:, :],
                                in_=ps[4][:, 0:512], func=AF.Copy), inc="a")
                            p.dma("sync", out_dram[ec * NET + et, :, qt * 512:(qt + 1) * 512],
                                  evac[:, 0:512] if EW >= 512 else evac[:, :])
                            p.do("scalar", lambda e: e.activation(
                                out=sqs[:, 0:512], in_=ps[4][:, 0:512], func=AF.Square), inc="a")
                            p.do("tensor", lambda e: e.matmul(
                                ps[5][0:1, 0:512], sb["ones128"][:, 0:1], sqs[:, 0:512],
                                start=True, stop=True), inc="t")
                            p.do("vector", lambda e, qt=qt: e.tensor_tensor(
                                nrm_row[0:1, qt * 512:(qt + 1) * 512],
                                nrm_row[0:1, qt * 512:(qt + 1) * 512],
                                ps[5][0:1, 0:512], op=ALU.add), inc="v")
                else:
                    for tt in range(S // 128):
                        def mams(e, ec=ec, tt=tt):
                            r = None
                            for ci in range(NC):
                                r = e.matmul(
                                    ps[4][:, 0:EW],
                                    xn[:, ci * S + tt * 128:ci * S + (tt + 1) * 128],
                                    wblk[:, ci * EW:(ci + 1) * EW],
                                    start=(ci == 0), stop=(ci == NC - 1))
                            return r
                        p.do("tensor", mams, inc="t")
                        p.do("scalar", lambda e: e.activation(
                            out=evac[:, 0:EW], in_=ps[4][:, 0:EW], func=AF.Copy), inc="a")
                        p.dma("sync", out_dram[tt, :, ec * EW:(ec + 1) * EW], evac[:, 0:EW])

        proj_block("fm", D, kdram, tok_n=S, nrm_row=nrm)
        proj_block("fm", 0, qdram, tok_n=T, nrm_row=nrmq)
        proj_block("tm", 2 * D, vdram)

        def bounds_v(e):
            e.reduce_max(mcell[0:1, 0:1], nrm[:, :], axis=AX.X, opt_input=False, opt_output=False)
            e.reduce_max(mcell[0:1, 1:2], nrmq[:, :], axis=AX.X, opt_input=False, opt_output=False)
            return e.tensor_scalar(mcell[0:1, 2:3], mcell[0:1, 1:2], mcell[0:1, 0:1], None, op0=ALU.mult)
        p.do("vector", bounds_v, inc="v")
        p.do("scalar", lambda e: e.activation(out=mcell[0:1, 3:4], in_=mcell[0:1, 2:3], func=AF.Sqrt), inc="a")
        p.do("vector", lambda e: e.tensor_scalar(mcell[0:1, 4:5], mcell[0:1, 3:4],
                                                 -scale, None, op0=ALU.mult), inc="v")
        p.do("tensor", lambda e: e.matmul(ps[5][:, 0:1], sb["onesk1"][0:1, :], mcell[0:1, 4:5],
                                          start=True, stop=True), inc="t")
        p.do("scalar", lambda e: e.activation(out=negm[:, :], in_=ps[5][:, 0:1], func=AF.Copy), inc="a")

        g_sb = sbuf("g", [128, NC * T], BF16)
        silu_sb = sbuf("silu", [128, 512], F32)

        for tq in range(T // 512):
            for hg in range(NHG):
                for ci in range(NC):
                    p.dma("gpsimd", wg_sb[:, ci * HGS:(ci + 1) * HGS],
                          wgT[ci * 128:(ci + 1) * 128, hg * HGS:(hg + 1) * HGS])
                    p.dma("gpsimd", wu_sb[:, ci * HGS:(ci + 1) * HGS],
                          wuT[ci * 128:(ci + 1) * 128, hg * HGS:(hg + 1) * HGS])
                for hc in range(NHC):
                    p.dma("gpsimd", wd_sb[:, hc * D:(hc + 1) * D],
                          wdT[hg * HGS + hc * 128: hg * HGS + (hc + 1) * 128, :])
                for hc in range(NHC):
                    def gatemm(e, tq=tq, hc=hc):
                        r = None
                        for ci in range(NC):
                            e.matmul(ps[4][:, 0:512],
                                     wg_sb[:, ci * HGS + hc * 128: ci * HGS + (hc + 1) * 128],
                                     xn[:, ci * S + tq * 512: ci * S + (tq + 1) * 512],
                                     start=(ci == 0), stop=(ci == NC - 1))
                            r = e.matmul(ps[5][:, 0:512],
                                         wu_sb[:, ci * HGS + hc * 128: ci * HGS + (hc + 1) * 128],
                                         xn[:, ci * S + tq * 512: ci * S + (tq + 1) * 512],
                                         start=(ci == 0), stop=(ci == NC - 1))
                        return r
                    p.do("tensor", gatemm, inc="t")
                    p.do("scalar", lambda e: e.activation(out=silu_sb[:, 0:512], in_=ps[4][:, 0:512],
                                                          func=AF.Silu, scale=float(DEQ)), inc="a")
                    p.do("vector", lambda e, hc=hc: e.scalar_tensor_tensor(
                        h_sb[:, hc * 512:(hc + 1) * 512], ps[5][:, 0:512], float(DEQ),
                        silu_sb[:, 0:512], op0=ALU.mult, op1=ALU.mult), inc="v")
                for dc in range(NC):
                    def dmm(e, dc=dc):
                        r = None
                        for hc in range(NHC):
                            r = e.matmul(ps[6 + dc % 2][:, 0:512],
                                         wd_sb[:, hc * D + dc * 128: hc * D + (dc + 1) * 128],
                                         h_sb[:, hc * 512:(hc + 1) * 512],
                                         start=(hc == 0), stop=(hc == NHC - 1))
                        return r
                    p.do("tensor", dmm, inc="t")

                    def yadd(e, dc=dc, hg=hg):
                        dst = y_sb[:, dc * 512:(dc + 1) * 512]
                        if hg == 0:
                            return e.tensor_copy(dst, ps[6 + dc % 2][:, 0:512])
                        return e.tensor_tensor(dst, dst, ps[6 + dc % 2][:, 0:512], op=ALU.add)
                    p.do("vector", yadd, inc="v")

            def gsilu(e, tq=tq):
                r = None
                for dc in range(NC):
                    r = e.activation(out=g_sb[:, dc * T + tq * 512: dc * T + (tq + 1) * 512],
                                     in_=y_sb[:, dc * 512:(dc + 1) * 512],
                                     func=AF.Silu, scale=float(DEQ))
                return r
            p.do("scalar", gsilu, inc="a")

        s_sb = ar2b[:, 2 * NC * 1024:2 * NC * 1024 + NC * T]
        rrow = sbuf("rrow", [1, 1024], F32)
        denr = sbuf("denr", [1, 1024], F32)

        NKB = S // 1024
        nqt = TH // 256
        for qh in range(2):
            for ci in range(NC):
                p.dma("sync", qh_sb[:, ci * TH:(ci + 1) * TH],
                      qdram[ci, :, qh * TH:(qh + 1) * TH])
            for kb in range(NKB):
                for ci in range(NC):
                    p.dma("sync", kb_sb[:, ci * 1024:(ci + 1) * 1024],
                          kdram[ci, :, kb * 1024:(kb + 1) * 1024])
                for kc8 in range(8):
                    p.dma("sync", vb_sb[:, kc8 * D:(kc8 + 1) * D], vdram[kb * 8 + kc8, :, :])
                for qt in range(nqt):
                    for kc in range(8):
                        def smm(e, kc=kc, qt=qt):
                            r = None
                            for ci in range(NC):
                                r = e.matmul(
                                    ps[4 + (kc % 2)][:, 0:256],
                                    kb_sb[:, ci * 1024 + kc * 128: ci * 1024 + (kc + 1) * 128],
                                    qh_sb[:, ci * TH + qt * 256: ci * TH + (qt + 1) * 256],
                                    start=(ci == 0), stop=(ci == NC - 1))
                            return r
                        p.do("tensor", smm, inc="t")
                        p.do("scalar", lambda e, kc=kc: e.activation(
                            out=pt_sb[:, kc * 256:(kc + 1) * 256], in_=ps[4 + (kc % 2)][:, 0:256],
                            func=AF.Exp, bias=negm[:, 0:1], scale=scale), inc="a")
                        p.do("tensor", lambda e, kc=kc: e.matmul(
                            ps[6][0:1, 0:256],
                            sb["ones128b"][:, 0:1], pt_sb[:, kc * 256:(kc + 1) * 256],
                            start=(kc == 0), stop=(kc == 7)), inc="t")
                    for grp in range(2):
                        def avmm(e, qt=qt, grp=grp):
                            r = None
                            for kc in range(8):
                                for dj in range(NC // 2):
                                    dc = grp * (NC // 2) + dj
                                    r = e.matmul(
                                        ps[dj][:, 0:256],
                                        vb_sb[:, kc * D + dc * 128: kc * D + (dc + 1) * 128],
                                        pt_sb[:, kc * 256:(kc + 1) * 256],
                                        start=(kc == 0), stop=(kc == 7))
                            return r
                        p.do("tensor", avmm, inc="t")

                        def sadd(e, qt=qt, kb=kb, grp=grp):
                            r = None
                            for dj in range(NC // 2):
                                dc = grp * (NC // 2) + dj
                                srcp = ps[dj][:, 0:256]
                                dst = sacc[:, dc * TH + qt * 256: dc * TH + (qt + 1) * 256]
                                if kb == 0:
                                    r = e.tensor_copy(dst, srcp)
                                else:
                                    r = e.tensor_tensor(dst, dst, srcp, op=ALU.add)
                            return r
                        p.do("vector", sadd, inc="v")

                    def dadd(e, qt=qt, kb=kb):
                        dd = denr[0:1, qt * 256:(qt + 1) * 256]
                        if kb == 0:
                            return e.tensor_copy(dd, ps[6][0:1, 0:256])
                        return e.tensor_tensor(dd, dd, ps[6][0:1, 0:256], op=ALU.add)
                    p.do("vector", dadd, inc="v")

            p.do("vector", lambda e: e.reciprocal(rrow[0:1, 0:nqt * 256], denr[0:1, 0:nqt * 256]), inc="v")
            for qt in range(nqt):
                p.do("tensor", lambda e, qt=qt: e.matmul(
                    ps[4][:, 0:256], sb["onesk1"][0:1, :], rrow[0:1, qt * 256:(qt + 1) * 256],
                    start=True, stop=True), inc="t")
                p.do("scalar", lambda e: e.activation(out=sqs[:, 0:256], in_=ps[4][:, 0:256],
                                                      func=AF.Copy), inc="a")

                def snorm(e, qh=qh, qt=qt):
                    r = None
                    for dc in range(NC):
                        r = e.tensor_tensor(
                            s_sb[:, dc * T + qh * TH + qt * 256: dc * T + qh * TH + (qt + 1) * 256],
                            sacc[:, dc * TH + qt * 256: dc * TH + (qt + 1) * 256],
                            sqs[:, 0:256], op=ALU.mult)
                    return r
                p.do("vector", snorm, inc="v")

        xres = sb["lnx"]
        for tq in range(T // 512):
            for ci in range(NC):
                p.dma("sync", xres[:, ci * 512:(ci + 1) * 512],
                      xb[ci * 128:(ci + 1) * 128, tq * 512:(tq + 1) * 512])

            def resid(e, tq=tq):
                r = None
                for ci in range(NC):
                    cs = slice(ci * T + tq * 512, ci * T + (tq + 1) * 512)
                    e.tensor_tensor(sqs[:, 0:512], s_sb[:, cs], g_sb[:, cs], op=ALU.mult)
                    r = e.tensor_tensor(xres[:, ci * 512:(ci + 1) * 512],
                                        xres[:, ci * 512:(ci + 1) * 512], sqs[:, 0:512], op=ALU.add)
                return r
            p.do("vector", resid, inc="v")
            for ci in range(NC):
                p.dma("scalar", xout[ci * 128:(ci + 1) * 128, tq * 512:(tq + 1) * 512],
                      xres[:, ci * 512:(ci + 1) * 512])

        if DEBUG_L0:
            for ci in range(NC):
                p.dma("gpsimd", dbg_xn[ci, :, :], xn[:, ci * S:(ci + 1) * S].bitcast(BF16))
                p.dma("gpsimd", dbg_s[ci, :, :], s_sb[:, ci * T:(ci + 1) * T])
                p.dma("gpsimd", dbg_g[ci, :, :], g_sb[:, ci * T:(ci + 1) * T])
            p.dma("gpsimd", dbg_k[:, :, :], kdram[:, :, :])
            p.dma("gpsimd", dbg_v[:, :, :], vdram[:, :, :])
            p.dma("gpsimd", dbg_q[:, :, :], qdram[:, :, :])
            p.dma("gpsimd", dbg_m[:, :], mcell[:, :])
        p.do("gpsimd", lambda e: e.nop(), inc="g")
        p.finalize(block)
    return nc


def _rot(a, h, T, axis):
    if h == 0:
        return np.ascontiguousarray(a)
    idx = list(range(T, 2 * T)) + list(range(0, T))
    return np.ascontiguousarray(np.take(a, idx, axis=axis))


def run_l0(x_full, W_qkv, w_gate, w_up, w_down, cfg=None, ncores=None):
    cfg = cfg or CFG
    B, S, D = x_full.shape
    T = S // 2
    consts = host_constants(cfg)
    nc = build_l0(cfg)
    ncores = ncores or min(8, 2 * B)
    wqkvT = np.ascontiguousarray(W_qkv.T.astype(np.float32))
    wgT = np.ascontiguousarray(w_gate.T)
    wuT = np.ascontiguousarray(w_up.T)
    wdT = np.ascontiguousarray(w_down.T)
    in_maps = []
    for c in range(ncores):
        b, h = c // 2, c % 2
        xb = _rot(x_full[b].T.astype(np.float32), h, T, axis=1)
        in_maps.append(dict(xb=xb, wqkvT=wqkvT, wgT=wgT, wuT=wuT, wdT=wdT,
                            c_ones128=consts["ones128"], c_onesk1=consts["onesk1"]))
    res = run_bass_kernel_spmd(nc, in_maps, core_ids=list(range(ncores)))
    out = np.zeros((B, S, D), np.float32)
    for c in range(ncores):
        b, h = c // 2, c % 2
        out[b, h * T:(h + 1) * T, :] = res.results[c]["xout"].T
    return out





class Arena:
    def __init__(self, ap_bf16, ap_f32, ncols_bf16):
        self.b = ap_bf16
        self.f = ap_f32
        self.n = ncols_bf16
        self.off = 0

    def reset(self):
        self.off = 0

    def take_b(self, cols):
        o = self.off
        assert o + cols <= self.n, f"arena overflow {o}+{cols}>{self.n}"
        self.off += cols
        return self.b[:, o:o + cols]

    def take_f(self, cols):
        if self.off % 2:
            self.off += 1
        o = self.off // 2
        self.off += 2 * cols
        assert self.off <= self.n, f"arena overflow {self.off}>{self.n}"
        return self.f[:, o:o + cols]


def emit_moe1(p, sb, ps, psb, A1, A2, xn, g_sb, cfg, HT, md):
    D, S = cfg["D"], cfg["S"]
    NC = D // 128
    T = S // 2
    ME, MS, MH = cfg["ME"], cfg["MS"], cfg["MH"]
    ES = ME * MS
    NEST = ES // 128
    NHT = max(1, MH // 128)
    HHW = min(128, MH)
    nc = p.nc

    A2.reset()
    gsb = sb["m_gsb"]
    bsb = sb["m_bsb"]
    p.dma("sync", gsb[:, :], md["gamma"])
    p.dma("sync", bsb[:, :], md["beta"])
    sb["lnx"] = A2.take_f(NC * 512)
    sb["lnsq"] = A2.take_f(NC * 512)
    sb["lnrow"] = A2.take_f(2048)[0:1, :]
    emit_ln(p, sb, ps, None, xn, S, gamma_beta=(gsb, bsb), x_sb_src=xn)
    if "dbg_xn2d" in sb:
        for ci in range(NC):
            p.dma("gpsimd", sb["dbg_xn2d"][ci, :, :], xn[:, ci * S:(ci + 1) * S])

    A2.reset()
    mu_sb = A2.take_b(NC * ES)
    disp = A2.take_b(S)
    NSPL = 16 if S >= 4096 else 8
    x2tmq = A2.take_b((S // NSPL // 128) * D)
    dtmq = A2.take_b((S // NSPL // 128) * 128)
    slot_acc = A2.take_f(NC * ES)
    slot_ob = slot_acc.bitcast(BF16)[:, 0:NC * ES]
    slot_b = A2.take_b(NC * ES)
    w1e = A2.take_b(NC * MH)
    w2e = A2.take_b(NHT * D)
    slot_sm = A2.take_b(NEST * D)
    comb_sm = A2.take_b(NEST * T)
    murow = A2.take_f(ES + 128)
    muinv = sb["m_muinv"]
    mubc = A2.take_f(ES)
    clf = A2.take_f(ES)
    clb = A2.take_b(ES)
    hs = A2.take_b(NHT * MS)
    sb["m_xbc"] = A2.take_f(512)
    sb["m_sq"] = A2.take_f(512)
    b1s = sb["m_b1s"]
    b2s = sb["m_b2s"]
    msc = sb["m_msc"]
    rowx = A2.take_f(640)
    sb["_moe"] = dict(disp=disp, slot_b=slot_b, slot_ob=slot_ob, comb_sm=comb_sm,
                      slot_sm=slot_sm, murow=murow, mubc=mubc)

    p.dma("sync", b1s[0:HHW, :], md["b1"])
    p.dma("sync", b2s[:, :], md["b2"])
    p.dma("sync", msc[0:1, 0:1], md["mscale"])
    for ci in range(NC):
        p.dma("gpsimd", mu_sb[:, ci * ES:(ci + 1) * ES],
              md["mu"][ci * 128:(ci + 1) * 128, :])

    sqm = sb["m_sq"]
    for ci in range(NC):
        p.do("scalar", lambda e, ci=ci: e.activation(
            out=sqm[:, 0:ES], in_=mu_sb[:, ci * ES:(ci + 1) * ES], func=AF.Square), inc="a")
        p.do("tensor", lambda e, ci=ci: e.matmul(
            ps[5][0:1, 0:ES], sb["ones128"][:, 0:1], sqm[:, 0:ES],
            start=(ci == 0), stop=(ci == NC - 1)), inc="t")
    p.do("scalar", lambda e: e.activation(out=murow[0:1, 0:ES], in_=ps[5][0:1, 0:ES],
                                          func=AF.Sqrt), inc="a")
    p.do("vector", lambda e: e.tensor_scalar(murow[0:1, 0:ES], murow[0:1, 0:ES], 1e-12,
                                             None, op0=ALU.max), inc="v")
    p.do("vector", lambda e: e.reciprocal(murow[0:1, 0:ES], murow[0:1, 0:ES]), inc="v")
    for est in range(NEST):
        p.do("tensor", lambda e, est=est: e.matmul(
            ps[5][:, 0:1], murow[0:1, est * 128:(est + 1) * 128], sb["ones128"][0:1, 0:1],
            start=True, stop=True), inc="t")
        p.do("scalar", lambda e, est=est: e.activation(
            out=muinv[:, est:est + 1], in_=ps[5][:, 0:1], func=AF.Copy), inc="a")
    p.do("tensor", lambda e: e.matmul(ps[6][:, 0:ES], sb["onesk1"][0:1, :], murow[0:1, 0:ES],
                                      start=True, stop=True), inc="t")
    p.do("scalar", lambda e: e.activation(out=mubc[:, 0:ES], in_=ps[6][:, 0:ES], func=AF.Copy), inc="a")

    SQ = S // NSPL
    NTT = max(1, SQ // 128)
    xbc = sb["m_xbc"]
    sqm = sb["m_sq"]
    for est in range(NEST):
        for qt in range(S // 512):
            for ci in range(NC):
                p.do("scalar", lambda e, ci=ci, qt=qt: e.activation(
                    out=sqm[:, 0:512], in_=xn[:, ci * S + qt * 512:ci * S + (qt + 1) * 512],
                    func=AF.Square), inc="a")
                p.do("tensor", lambda e, ci=ci: e.matmul(
                    ps[5][0:1, 0:512], sb["ones128"][:, 0:1], sqm[:, 0:512],
                    start=(ci == 0), stop=(ci == NC - 1)), inc="t")
            p.do("scalar", lambda e: e.activation(out=rowx[0:1, 0:512], in_=ps[5][0:1, 0:512],
                                                  func=AF.Sqrt), inc="a")
            p.do("vector", lambda e: e.tensor_scalar(rowx[0:1, 0:512], rowx[0:1, 0:512],
                                                     1e-12, None, op0=ALU.max), inc="v")
            p.do("vector", lambda e: e.reciprocal(rowx[0:1, 0:512], rowx[0:1, 0:512]), inc="v")
            p.do("vector", lambda e: e.tensor_scalar(rowx[0:1, 0:512], rowx[0:1, 0:512],
                                                     msc[0:1, 0:1], None, op0=ALU.mult), inc="v")
            p.do("tensor", lambda e: e.matmul(ps[6][:, 0:512], sb["onesk1"][0:1, :],
                                              rowx[0:1, 0:512], start=True, stop=True), inc="t")
            p.do("scalar", lambda e: e.activation(out=xbc[:, 0:512], in_=ps[6][:, 0:512],
                                                  func=AF.Copy), inc="a")
            def lmm(e, est=est, qt=qt):
                r = None
                for ci in range(NC):
                    r = e.matmul(ps[4][:, 0:512],
                                 mu_sb[:, ci * ES + est * 128:ci * ES + (est + 1) * 128],
                                 xn[:, ci * S + qt * 512:ci * S + (qt + 1) * 512],
                                 start=(ci == 0), stop=(ci == NC - 1))
                return r
            p.do("tensor", lmm, inc="t")
            p.do("vector", lambda e, est=est, qt=qt: e.scalar_tensor_tensor(
                disp[:, qt * 512:(qt + 1) * 512],
                ps[4][:, 0:512], muinv[:, est:est + 1], xbc[:, 0:512],
                op0=ALU.mult, op1=ALU.mult), inc="v")
        p.do("vector", lambda e: e.reduce_max(
            sb["m_mx"][:, 0:1], disp[:, :], axis=AX.X, opt_input=False, opt_output=False), inc="v")
        p.do("vector", lambda e: e.tensor_scalar(sb["m_mx"][:, 0:2], sb["m_mx"][:, 0:2],
                                                 -1.0, None, op0=ALU.mult), inc="v")
        p.do("scalar", lambda e: e.activation(
            out=disp[:, :], in_=disp[:, :], func=AF.Exp, bias=sb["m_mx"][:, 0:1],
            accum_out=sb["m_sm"][:, 0:1]), inc="a")
        p.do("vector", lambda e: e.reciprocal(sb["m_sm"][:, 0:2], sb["m_sm"][:, 0:2]), inc="v")
        p.do("vector", lambda e: e.tensor_scalar(disp[:, :], disp[:, :], sb["m_sm"][:, 0:1], None,
                                                 op0=ALU.mult), inc="v")
        if "dbg_cells" in sb and est == 0:
            p.dma("gpsimd", sb["dbg_cells"][0:128, 0:2], sb["m_mx"][:, 0:2])
            p.dma("gpsimd", sb["dbg_cells"][0:128, 2:4], sb["m_sm"][:, 0:2])
            p.dma("gpsimd", sb["dbg_cells"][0:128, 4:8], disp[:, 0:4])
        if "dbg_disp" in sb:
            p.dma("gpsimd", sb["dbg_disp"][est, :, :], disp[:, :])
        for q4 in range(NSPL):
            for ci in range(NC):
                xt = x2tmq.rearrange("p (a b) -> p a b", b=D)[:, :, ci * 128:(ci + 1) * 128]
                p.dma("scalar", xt, xn[:, ci * S + q4 * SQ:ci * S + (q4 + 1) * SQ],
                      transpose=True, sem="d_sc")
            dt_ = dtmq.rearrange("p (a b) -> p a b", b=128)[:, :, 0:128]
            p.dma("scalar", dt_, disp[:, q4 * SQ:(q4 + 1) * SQ],
                  transpose=True, sem="d_sc")
            for dpass in range(2):
                nd = NC // 2
                def simm(e, q4=q4, dpass=dpass, nd=nd, est=est):
                    r = None
                    for tt in range(NTT):
                        for dj in range(nd):
                            dt2 = dpass * nd + dj
                            r = e.matmul(ps[dj][:, est * 128:(est + 1) * 128],
                                         x2tmq[:, tt * D + dt2 * 128:tt * D + (dt2 + 1) * 128],
                                         dtmq[:, tt * 128:(tt + 1) * 128],
                                         start=(tt == 0), stop=(tt == NTT - 1))
                    return r
                p.do("tensor", simm, inc="t")
                def sacc_add(e, q4=q4, dpass=dpass, nd=nd, est=est):
                    r = None
                    for dj in range(nd):
                        dt2 = dpass * nd + dj
                        dst = slot_acc[:, dt2 * ES + est * 128:dt2 * ES + (est + 1) * 128]
                        if q4 == 0:
                            r = e.tensor_copy(dst, ps[dj][:, est * 128:(est + 1) * 128])
                        else:
                            r = e.tensor_tensor(dst, dst, ps[dj][:, est * 128:(est + 1) * 128],
                                                op=ALU.add)
                    return r
                p.do("vector", sacc_add, inc="v")
    p.do("vector", lambda e: e.tensor_copy(slot_b[:, :], slot_acc[:, :]), inc="v")

    for ex in range(ME):
        for ci in range(NC):
            p.dma("gpsimd", w1e[:, ci * MH:(ci + 1) * MH],
                  md["w1"][ex, ci * 128:(ci + 1) * 128, :])
        for ht in range(NHT):
            p.dma("gpsimd", w2e[0:HHW, ht * D:(ht + 1) * D],
                  md["w2"][ex, ht * 128:ht * 128 + HHW, :])
        for ht in range(NHT):
            def hmm(e, ex=ex, ht=ht):
                r = None
                for ci in range(NC):
                    r = e.matmul(ps[4][0:HHW, 0:MS],
                                 w1e[:, ci * MH + ht * 128:ci * MH + ht * 128 + HHW],
                                 slot_b[:, ci * ES + ex * MS:ci * ES + (ex + 1) * MS],
                                 start=(ci == 0), stop=(ci == NC - 1))
                return r
            p.do("tensor", hmm, inc="t")
            p.do("scalar", lambda e, ex=ex, ht=ht: e.activation(
                out=hs[0:HHW, ht * MS:(ht + 1) * MS], in_=ps[4][0:HHW, 0:MS], func=AF.Gelu,
                bias=b1s[0:HHW, ex * NHT + ht:ex * NHT + ht + 1]), inc="a")
        for dt2 in range(NC):
            def omm(e, ex=ex, dt2=dt2):
                r = None
                for ht in range(NHT):
                    r = e.matmul(ps[5][:, 0:MS],
                                 w2e[0:HHW, ht * D + dt2 * 128:ht * D + (dt2 + 1) * 128],
                                 hs[0:HHW, ht * MS:(ht + 1) * MS],
                                 start=(ht == 0), stop=(ht == NHT - 1))
                return r
            p.do("tensor", omm, inc="t")
            p.do("vector", lambda e, ex=ex, dt2=dt2: e.tensor_scalar(
                slot_ob[:, dt2 * ES + ex * MS:dt2 * ES + (ex + 1) * MS],
                ps[5][:, 0:MS], b2s[:, ex * NC + dt2:ex * NC + dt2 + 1], None,
                op0=ALU.add), inc="v")

    for dt2 in range(NC):
        for est in range(NEST):
            p.do("tensor", lambda e, dt2=dt2, est=est: e.transpose(
                psb[:, 0:128], slot_ob[:, dt2 * ES + est * 128:dt2 * ES + (est + 1) * 128],
                sb["id128"][:, :]), inc="t")
            p.do("scalar", lambda e, dt2=dt2, est=est: e.activation(
                out=slot_sm[:, est * D + dt2 * 128:est * D + (dt2 + 1) * 128],
                in_=psb[:, 0:128], func=AF.Copy), inc="a")

    for tt in range(T // 128):
        tok0 = HT + tt * 128
        for ci in range(NC):
            p.do("scalar", lambda e, ci=ci, tok0=tok0: e.activation(
                out=sqm[:, 0:128], in_=xn[:, ci * S + tok0:ci * S + tok0 + 128],
                func=AF.Square), inc="a")
            p.do("tensor", lambda e, ci=ci: e.matmul(
                ps[5][0:1, 0:128], sb["ones128"][:, 0:1], sqm[:, 0:128],
                start=(ci == 0), stop=(ci == NC - 1)), inc="t")
        p.do("scalar", lambda e: e.activation(out=rowx[0:1, 0:128], in_=ps[5][0:1, 0:128],
                                              func=AF.Sqrt), inc="a")
        p.do("vector", lambda e: e.tensor_scalar(rowx[0:1, 0:128], rowx[0:1, 0:128], 1e-12,
                                                 None, op0=ALU.max), inc="v")
        p.do("vector", lambda e: e.reciprocal(rowx[0:1, 0:128], rowx[0:1, 0:128]), inc="v")
        p.do("vector", lambda e: e.tensor_scalar(rowx[0:1, 0:128], rowx[0:1, 0:128],
                                                 msc[0:1, 0:1], None, op0=ALU.mult), inc="v")
        p.do("tensor", lambda e: e.matmul(ps[6][:, 0:1], rowx[0:1, 0:128],
                                          sb["ones128"][0:1, 0:1], start=True, stop=True), inc="t")
        p.do("scalar", lambda e: e.activation(out=sb["m_xc"][:, 0:1], in_=ps[6][:, 0:1],
                                              func=AF.Copy), inc="a")
        def cmm(e, tok0=tok0):
            r = None
            for ci in range(NC):
                r = e.matmul(ps[4][:, 0:ES],
                             xn[:, ci * S + tok0:ci * S + tok0 + 128],
                             mu_sb[:, ci * ES:(ci + 1) * ES],
                             start=(ci == 0), stop=(ci == NC - 1))
            return r
        p.do("tensor", cmm, inc="t")
        p.do("vector", lambda e: e.scalar_tensor_tensor(
            clf[:, 0:ES], ps[4][:, 0:ES], sb["m_xc"][:, 0:1], mubc[:, 0:ES],
            op0=ALU.mult, op1=ALU.mult), inc="v")
        p.do("vector", lambda e: e.reduce_max(sb["m_mx"][:, 0:1], clf[:, 0:ES], axis=AX.X,
                                              opt_input=False, opt_output=False), inc="v")
        p.do("vector", lambda e: e.tensor_scalar(sb["m_mx"][:, 0:2], sb["m_mx"][:, 0:2], -1.0,
                                                 None, op0=ALU.mult), inc="v")
        p.do("scalar", lambda e: e.activation(out=clb[:, 0:ES], in_=clf[:, 0:ES], func=AF.Exp,
                                              bias=sb["m_mx"][:, 0:1],
                                              accum_out=sb["m_sm"][:, 0:1]), inc="a")
        p.do("vector", lambda e: e.reciprocal(sb["m_sm"][:, 0:2], sb["m_sm"][:, 0:2]), inc="v")
        p.do("vector", lambda e: e.tensor_scalar(clb[:, 0:ES], clb[:, 0:ES], sb["m_sm"][:, 0:1],
                                                 None, op0=ALU.mult), inc="v")
        for est in range(NEST):
            p.do("tensor", lambda e, est=est: e.transpose(
                psb[:, 0:128], clb[:, est * 128:(est + 1) * 128], sb["id128"][:, :]), inc="t")
            p.do("scalar", lambda e, est=est, tt=tt: e.activation(
                out=comb_sm[:, est * T + tt * 128:est * T + (tt + 1) * 128],
                in_=psb[:, 0:128], func=AF.Copy), inc="a")

    for dt2 in range(NC):
        for qt in range(T // 512):
            def gmm(e, dt2=dt2, qt=qt):
                r = None
                for est in range(NEST):
                    r = e.matmul(ps[4][:, 0:512],
                                 slot_sm[:, est * D + dt2 * 128:est * D + (dt2 + 1) * 128],
                                 comb_sm[:, est * T + qt * 512:est * T + (qt + 1) * 512],
                                 start=(est == 0), stop=(est == NEST - 1))
                return r
            p.do("tensor", gmm, inc="t")
            p.do("scalar", lambda e, dt2=dt2, qt=qt: e.activation(
                out=g_sb[:, dt2 * T + qt * 512:dt2 * T + (qt + 1) * 512],
                in_=ps[4][:, 0:512], func=AF.Silu), inc="a")



def build_l1(cfg, h):
    D, S = cfg["D"], cfg["S"]
    NC = D // 128
    T = S // 2
    NH, DH = cfg["NH1"], cfg["DH1"]
    NHP = NH // 2
    ME, MS, MH = cfg["ME"], cfg["MS"], cfg["MH"]
    ES = ME * MS
    NEST = ES // 128
    NHT = max(1, MH // 128)
    HHW = min(128, MH)
    EW = min(512, D)
    NET = EW // 128
    NECH = max(1, D // EW)
    scale = float(np.float32(DH) ** -0.5)
    NKC = S // 128
    HT = h * T

    nc = bass.Bass("TRN2", target_bir_lowering=False)
    stack = contextlib.ExitStack()

    xb = nc.dram_tensor("xb", [D, S], F32, kind="ExternalInput").ap()
    wqkv = nc.dram_tensor("wqkv", [D, 3 * D], F32, kind="ExternalInput").ap()
    wout = nc.dram_tensor("wout", [D, D], F32, kind="ExternalInput").ap()
    cosd = nc.dram_tensor("cosd", [128, S], F32, kind="ExternalInput").ap()
    sind = nc.dram_tensor("sind", [128, S], F32, kind="ExternalInput").ap()
    shiftp_d = nc.dram_tensor("shiftp_w", [128, 128], BF16, kind="ExternalInput").ap()
    dmask_d = nc.dram_tensor("dmask_w", [4, 128, 512], BF16, kind="ExternalInput").ap()
    gamma_d = nc.dram_tensor("m_gamma", [128, NC], F32, kind="ExternalInput").ap()
    beta_d = nc.dram_tensor("m_beta", [128, NC], F32, kind="ExternalInput").ap()
    mu_d = nc.dram_tensor("m_mu", [D, ES], F32, kind="ExternalInput").ap()
    mscale_d = nc.dram_tensor("m_scale", [1, 1], F32, kind="ExternalInput").ap()
    w1_d = nc.dram_tensor("m_w1", [ME, D, MH], F32, kind="ExternalInput").ap()
    b1_d = nc.dram_tensor("m_b1", [HHW, ME * NHT], F32, kind="ExternalInput").ap()
    w2_d = nc.dram_tensor("m_w2", [ME, MH, D], F32, kind="ExternalInput").ap()
    b2_d = nc.dram_tensor("m_b2", [128, ME * NC], F32, kind="ExternalInput").ap()
    ones128_d = nc.dram_tensor("c_ones128", [128, 1], F32, kind="ExternalInput").ap()
    onesk1_d = nc.dram_tensor("c_onesk1", [1, 128], F32, kind="ExternalInput").ap()
    elow_d = nc.dram_tensor("c_elow", [1, 128], F32, kind="ExternalInput").ap()
    ehigh_d = nc.dram_tensor("c_ehigh", [1, 128], F32, kind="ExternalInput").ap()
    id128_d = nc.dram_tensor("c_id128w", [128, 128], BF16, kind="ExternalInput").ap()
    xout = nc.dram_tensor("xout", [D, T], F32, kind="ExternalOutput").ap()
    kdram = nc.dram_tensor("kdram", [NC, 128, S], BF16)
    vdram = nc.dram_tensor("vdram", [S // 128, 128, D], BF16)
    qdram = nc.dram_tensor("qdram", [NC, 128, T], BF16)
    hn_dram = nc.dram_tensor("hn_dram", [2, NH], F32)
    if DEBUG_L1:
        dbg_cells = nc.dram_tensor("dbg_cells", [128, 8], F32, kind="ExternalOutput").ap()
        dbg_row = nc.dram_tensor("dbg_row", [2, 512], F32, kind="ExternalOutput").ap()
        dbg_xn2 = nc.dram_tensor("dbg_xn2", [NC, 128, S], BF16, kind="ExternalOutput").ap()
        dbg_k = nc.dram_tensor("dbg_k", [NC, 128, S], BF16, kind="ExternalOutput").ap()
        dbg_q = nc.dram_tensor("dbg_q", [NC, 128, T], BF16, kind="ExternalOutput").ap()
        dbg_v = nc.dram_tensor("dbg_v", [S // 128, 128, D], BF16, kind="ExternalOutput").ap()
        dbg_m = nc.dram_tensor("dbg_m", [1, 64], F32, kind="ExternalOutput").ap()
        dbg_s0 = nc.dram_tensor("dbg_s0", [NC, 128, T], BF16, kind="ExternalOutput").ap()
        dbg_s = nc.dram_tensor("dbg_s", [NC, 128, T], BF16, kind="ExternalOutput").ap()
        dbg_g = nc.dram_tensor("dbg_g", [NC, 128, T], BF16, kind="ExternalOutput").ap()
        dbg_disp = nc.dram_tensor("dbg_disp", [2, 128, S], BF16, kind="ExternalOutput").ap()
        dbg_slot = nc.dram_tensor("dbg_slot", [NC, 128, 256], BF16, kind="ExternalOutput").ap()
        dbg_sob = nc.dram_tensor("dbg_sob", [NC, 128, 256], BF16, kind="ExternalOutput").ap()
        dbg_comb = nc.dram_tensor("dbg_comb", [2, 128, T], BF16, kind="ExternalOutput").ap()

    sb = {}
    with stack:
        block = stack.enter_context(nc.Block())
        ps = [stack.enter_context(nc.psum_tensor(f"ps{i}", [128, 512], F32)) for i in range(7)]
        psb = stack.enter_context(nc.psum_tensor("psb", [128, 1024], BF16))

        def sbuf(name, shape, dt):
            t = stack.enter_context(nc.sbuf_tensor(name, shape, dt))
            sb[name] = t
            return t

        sbuf("ones128", [128, 1], F32)
        sbuf("ones128b", [128, 1], BF16)
        sbuf("onesk1", [1, 128], F32)
        sbuf("elow", [1, 128], F32)
        sbuf("ehigh", [1, 128], F32)
        id128 = sbuf("id128", [128, 128], BF16)
        shiftp = sbuf("shiftp", [128, 128], BF16)
        evac = sbuf("evac", [128, 512], BF16)
        evac2 = sbuf("evac2", [128, 512], BF16)
        sqs = sbuf("sqs", [128, 512], F32)
        sqs2 = sbuf("sqs2", [128, 512], F32)
        mcell = sbuf("mcell", [1, 64], F32)
        negmh = sbuf("negmh", [128, NH], F32)
        s0_sb = sbuf("s0", [128, NC * T], BF16)
        g_sb = sbuf("g", [128, NC * T], BF16)
        sbuf("m_gsb", [128, NC], F32)
        sbuf("m_bsb", [128, NC], F32)
        sbuf("m_muinv", [128, NEST], F32)
        sbuf("m_b1s", [128, ME * NHT], F32)
        sbuf("m_b2s", [128, ME * NC], F32)
        sbuf("m_msc", [1, 2], F32)
        sbuf("m_mx", [128, 2], F32)
        sbuf("m_sm", [128, 2], F32)
        sbuf("m_xc", [128, 2], F32)
        if DEBUG_L1:
            sb["dbg_cells"] = dbg_cells
            sb["dbg_row"] = dbg_row
            sb["dbg_xn2d"] = dbg_xn2
            sb["dbg_disp"] = dbg_disp

        A1N = max(NC * S, 2 * S + T + 3072)
        ar1 = sbuf("ar1", [128, A1N], BF16)
        ar1f = ar1[:, :].bitcast(F32)
        A2F = 16000 if D == 1024 else 5440
        ar2 = sbuf("ar2", [128, 2 * A2F], BF16)
        ar2f = ar2[:, :].bitcast(F32)
        A1 = Arena(ar1[:, :], ar1f, 2 * (A1N // 2) if False else A1N)
        A2 = Arena(ar2[:, :], ar2f, 2 * A2F)

        p = Prog(nc, stack)
        p.dma("sync", sb["ones128"][:, :], ones128_d)
        p.dma("gpsimd", sb["ones128b"][:, :], ones128_d)
        p.dma("sync", sb["onesk1"][:, :], onesk1_d)
        p.dma("sync", sb["elow"][:, :], elow_d)
        p.dma("sync", sb["ehigh"][:, :], ehigh_d)
        p.dma("sync", id128[:, :], id128_d)
        p.dma("sync", shiftp[:, :], shiftp_d)

        xn = A1.take_b(NC * S)
        A2.reset()
        sb["lnx"] = A2.take_f(NC * 512)
        sb["lnsq"] = A2.take_f(NC * 512)
        sb["lnrow"] = A2.take_f(2048)[0:1, :]
        emit_ln(p, sb, ps, xb, xn, S)

        A2.reset()
        wblk = A2.take_b(NC * EW)
        cos_sb = A2.take_f(S)
        sin_sb = A2.take_f(S)
        nhx = A2.take_f(2 * (8 * (S // 512) + 2))
        p.dma("sync", cos_sb[:, :], cosd)
        p.dma("sync", sin_sb[:, :], sind)

        def proj1(kind, e0, out_dram, tok0, tok_n, rope, hn_slot):
            ntile = tok_n // 512
            for ec in range(NECH):
                for ci in range(NC):
                    p.dma("gpsimd", wblk[:, ci * EW:(ci + 1) * EW],
                          wqkv[ci * 128:(ci + 1) * 128, e0 + ec * EW: e0 + (ec + 1) * EW])
                if kind == "fm":
                    for et in range(NET):
                        och = ec * NET + et
                        for qt in range(ntile):
                            def mams(e, ec=ec, et=et, qt=qt):
                                r = None
                                for ci in range(NC):
                                    r = e.matmul(
                                        ps[4][:, 0:512],
                                        wblk[:, ci * EW + et * 128: ci * EW + (et + 1) * 128],
                                        xn[:, ci * S + tok0 + qt * 512:ci * S + tok0 + (qt + 1) * 512],
                                        start=(ci == 0), stop=(ci == NC - 1))
                                return r
                            p.do("tensor", mams, inc="t")
                            p.do("scalar", lambda e: e.activation(
                                out=sqs[:, 0:512], in_=ps[4][:, 0:512], func=AF.Square), inc="a")
                            p.do("tensor", lambda e: e.matmul(
                                ps[5][0:2, 0:512], sb["ones64x2"][:, 0:2], sqs[:, 0:512],
                                start=True, stop=True), inc="t")
                            p.do("vector", lambda e, och=och, qt=qt: e.reduce_max(
                                nhx[0:2, och * ntile + qt:och * ntile + qt + 1][0:2, 0:1],
                                ps[5][0:2, 0:512], axis=AX.X, opt_input=False, opt_output=False), inc="v")
                            p.do("scalar", lambda e: e.activation(
                                out=evac[:, 0:512], in_=ps[4][:, 0:512], func=AF.Copy), inc="a")
                            if rope:
                                p.do("tensor", lambda e: e.matmul(
                                    ps[6][:, 0:512], shiftp[:, :], evac[:, 0:512],
                                    start=True, stop=True), inc="t")
                                cslice = slice(tok0 + qt * 512, tok0 + (qt + 1) * 512)
                                def ropemix(e, cslice=cslice):
                                    e.tensor_tensor(sqs[:, 0:512], evac[:, 0:512],
                                                    cos_sb[:, cslice], op=ALU.mult)
                                    e.tensor_tensor(sqs2[:, 0:512], ps[6][:, 0:512],
                                                    sin_sb[:, cslice], op=ALU.mult)
                                    return e.tensor_tensor(evac2[:, 0:512], sqs[:, 0:512],
                                                           sqs2[:, 0:512], op=ALU.add)
                                p.do("vector", ropemix, inc="v")
                                out_sb = evac2
                            else:
                                out_sb = evac
                            p.dma("sync", out_dram[och, :, qt * 512:(qt + 1) * 512], out_sb[:, 0:512])
                        NFX = 8 * (S // 512)
                        def foldmax(e, och=och, ntile=ntile, NFX=NFX):
                            return e.reduce_max(nhx[0:2, NFX:NFX + 1],
                                                nhx[0:2, och * ntile:och * ntile + ntile], axis=AX.X, opt_input=False, opt_output=False)
                        p.do("vector", foldmax, inc="v")
                        p.dma("sync", hn_dram[hn_slot, och * 2:och * 2 + 2],
                              nhx[0:2, NFX:NFX + 1])
                else:
                    for tt in range(S // 128):
                        def mams(e, ec=ec, tt=tt):
                            r = None
                            for ci in range(NC):
                                r = e.matmul(
                                    ps[4][:, 0:EW],
                                    xn[:, ci * S + tt * 128:ci * S + (tt + 1) * 128],
                                    wblk[:, ci * EW:(ci + 1) * EW],
                                    start=(ci == 0), stop=(ci == NC - 1))
                            return r
                        p.do("tensor", mams, inc="t")
                        p.do("scalar", lambda e: e.activation(
                            out=evac[:, 0:EW], in_=ps[4][:, 0:EW], func=AF.Copy), inc="a")
                        p.dma("sync", out_dram[tt, :, ec * EW:(ec + 1) * EW], evac[:, 0:EW])

        sbuf("ones64x2", [128, 2], F32)
        o64_d = nc.dram_tensor("c_ones64x2", [128, 2], F32, kind="ExternalInput").ap()
        p.dma("sync", sb["ones64x2"][:, :], o64_d)

        proj1("fm", D, kdram, 0, S, True, 0)
        proj1("fm", 0, qdram, HT, T, True, 1)
        proj1("tm", 2 * D, vdram, 0, S, False, None)

        p.dma("sync", mcell[0:1, 0:NH], hn_dram[0, :])
        p.dma("sync", mcell[0:1, NH:2 * NH], hn_dram[1, :])
        p.do("vector", lambda e: e.tensor_tensor(mcell[0:1, 0:NH], mcell[0:1, 0:NH],
                                                 mcell[0:1, NH:2 * NH], op=ALU.mult), inc="v")
        p.do("scalar", lambda e: e.activation(out=mcell[0:1, 0:NH], in_=mcell[0:1, 0:NH], func=AF.Sqrt), inc="a")
        p.do("vector", lambda e: e.tensor_scalar(mcell[0:1, NH:2 * NH], mcell[0:1, 0:NH],
                                                 -scale, None, op0=ALU.mult), inc="v")
        p.do("tensor", lambda e: e.matmul(ps[5][:, 0:NH], sb["onesk1"][0:1, :],
                                          mcell[0:1, NH:2 * NH], start=True, stop=True), inc="t")
        p.do("scalar", lambda e: e.activation(out=negmh[:, :], in_=ps[5][:, 0:NH], func=AF.Copy), inc="a")

        emit_moe1(p, sb, ps, psb, A1, A2, xn, g_sb, cfg, HT,
                  dict(gamma=gamma_d, beta=beta_d, mu=mu_d, mscale=mscale_d,
                       w1=w1_d, b1=b1_d, w2=w2_d, b2=b2_d))

        A1.reset()
        khp = A1.take_b(S)
        vhp = A1.take_b(S)
        qhp = A1.take_b(T)
        ptA = A1.take_b(512)
        ptB = A1.take_b(512)
        rb = A1.take_f(512)
        A2.reset()
        dmask = A2.take_b(4 * 512)
        rrow = A2.take_f(1024)[0:1, :]
        for r in range(4):
            p.dma("sync", dmask[:, r * 512:(r + 1) * 512], dmask_d[r, :, :])

        for hp in range(NHP):
            p.dma("sync", khp[:, :], kdram[hp, :, :])
            p.dma("sync", qhp[:, :], qdram[hp, :, :])
            for tt in range(S // 128):
                p.dma("sync", vhp[:, tt * 128:(tt + 1) * 128],
                      vdram[tt, :, hp * 128:(hp + 1) * 128])
            for qt in range(T // 512):
                qglob = HT + qt * 512
                next_ = qglob + 512
                ext = next_ // 128
                for kc in range(ext):
                    diag = kc * 128 >= qglob
                    rmask = (kc * 128 - qglob) // 128 if diag else None
                    def smm(e, hp=hp, qt=qt, kc=kc):
                        e.matmul(ps[4][:, 0:512],
                                 khp[0:64, kc * 128:(kc + 1) * 128],
                                 qhp[0:64, qt * 512:(qt + 1) * 512],
                                 start=True, stop=True)
                        return e.matmul(ps[5][:, 0:512],
                                        khp[64:128, kc * 128:(kc + 1) * 128],
                                        qhp[64:128, qt * 512:(qt + 1) * 512],
                                        start=True, stop=True, tile_position=(64, 0))
                    p.do("tensor", smm, inc="t")
                    def expab(e, hp=hp, kc=kc):
                        e.activation(out=ptA[:, 0:512], in_=ps[4][:, 0:512], func=AF.Exp,
                                     bias=negmh[:, 2 * hp:2 * hp + 1], scale=scale)
                        return e.activation(out=ptB[:, 0:512], in_=ps[5][:, 0:512], func=AF.Exp,
                                            bias=negmh[:, 2 * hp + 1:2 * hp + 2], scale=scale)
                    p.do("scalar", expab, inc="a")
                    if diag:
                        def maskab(e, rmask=rmask):
                            e.tensor_tensor(ptA[:, 0:512], ptA[:, 0:512],
                                            dmask[:, rmask * 512:(rmask + 1) * 512], op=ALU.mult)
                            return e.tensor_tensor(ptB[:, 0:512], ptB[:, 0:512],
                                                   dmask[:, rmask * 512:(rmask + 1) * 512], op=ALU.mult)
                        p.do("vector", maskab, inc="v")
                    def avmm(e, kc=kc, ext=ext):
                        e.matmul(ps[0][0:64, 0:512], vhp[:, kc * 128:kc * 128 + 64],
                                 ptA[:, 0:512], start=(kc == 0), stop=(kc == ext - 1))
                        e.matmul(ps[1][64:128, 0:512], vhp[:, kc * 128 + 64:(kc + 1) * 128],
                                 ptB[:, 0:512], start=(kc == 0), stop=(kc == ext - 1),
                                 tile_position=(0, 64))
                        e.matmul(ps[2][0:1, 0:512], sb["ones128b"][:, 0:1], ptA[:, 0:512],
                                 start=(kc == 0), stop=(kc == ext - 1))
                        return e.matmul(ps[3][0:1, 0:512], sb["ones128b"][:, 0:1], ptB[:, 0:512],
                                        start=(kc == 0), stop=(kc == ext - 1))
                    p.do("tensor", avmm, inc="t")
                def recips(e):
                    e.reciprocal(rrow[0:1, 0:512], ps[2][0:1, 0:512])
                    return e.reciprocal(rrow[0:1, 512:1024], ps[3][0:1, 0:512])
                p.do("vector", recips, inc="v")
                def rbc(e):
                    e.matmul(ps[6][:, 0:512], sb["elow"][0:1, :], rrow[0:1, 0:512],
                             start=True, stop=True)
                    return e.matmul(ps[6][:, 0:512], sb["ehigh"][0:1, :], rrow[0:1, 512:1024],
                                    start=False, stop=True)
                p.do("tensor", rbc, inc="t")
                p.do("scalar", lambda e: e.activation(out=rb[:, 0:512], in_=ps[6][:, 0:512],
                                                      func=AF.Copy), inc="a")
                def s0ev(e, hp=hp, qt=qt):
                    cs = slice(hp * T + qt * 512, hp * T + (qt + 1) * 512)
                    e.tensor_tensor(s0_sb[0:64, cs], ps[0][0:64, 0:512], rb[0:64, 0:512], op=ALU.mult)
                    return e.tensor_tensor(s0_sb[64:128, cs], ps[1][64:128, 0:512],
                                           rb[64:128, 0:512], op=ALU.mult)
                p.do("vector", s0ev, inc="v")

        A1.reset()
        s_sb = A1.take_b(NC * T)
        woutb = A2
        A2.reset()
        wo_sb = A2.take_b(NC * 128)
        for ot in range(NC):
            for ci in range(NC):
                p.dma("gpsimd", wo_sb[:, ci * 128:(ci + 1) * 128],
                      wout[ci * 128:(ci + 1) * 128, ot * 128:(ot + 1) * 128])
            for qt in range(T // 512):
                def omm(e, ot=ot, qt=qt):
                    r = None
                    for ci in range(NC):
                        r = e.matmul(ps[4][:, 0:512],
                                     wo_sb[:, ci * 128:(ci + 1) * 128],
                                     s0_sb[:, ci * T + qt * 512:ci * T + (qt + 1) * 512],
                                     start=(ci == 0), stop=(ci == NC - 1))
                    return r
                p.do("tensor", omm, inc="t")
                p.do("scalar", lambda e, ot=ot, qt=qt: e.activation(
                    out=s_sb[:, ot * T + qt * 512:ot * T + (qt + 1) * 512],
                    in_=ps[4][:, 0:512], func=AF.Copy), inc="a")

        xres = sbuf("xres", [128, 512], F32)
        for tq in range(T // 512):
            for ci in range(NC):
                p.dma("sync", xres[:, 0:512],
                      xb[ci * 128:(ci + 1) * 128, HT + tq * 512:HT + (tq + 1) * 512])
                def resid(e, tq=tq, ci=ci):
                    cs = slice(ci * T + tq * 512, ci * T + (tq + 1) * 512)
                    e.tensor_tensor(sqs[:, 0:512], s_sb[:, cs], g_sb[:, cs], op=ALU.mult)
                    return e.tensor_tensor(xres[:, 0:512], xres[:, 0:512], sqs[:, 0:512], op=ALU.add)
                p.do("vector", resid, inc="v")
                p.dma("scalar", xout[ci * 128:(ci + 1) * 128, tq * 512:(tq + 1) * 512],
                      xres[:, 0:512])

        if DEBUG_L1:
            for ci in range(NC):
                p.dma("gpsimd", dbg_s0[ci, :, :], s0_sb[:, ci * T:(ci + 1) * T])
                p.dma("gpsimd", dbg_s[ci, :, :], s_sb[:, ci * T:(ci + 1) * T])
                p.dma("gpsimd", dbg_g[ci, :, :], g_sb[:, ci * T:(ci + 1) * T])
            p.dma("gpsimd", dbg_k[:, :, :], kdram[:, :, :])
            p.dma("gpsimd", dbg_q[:, :, :], qdram[:, :, :])
            p.dma("gpsimd", dbg_v[:, :, :], vdram[:, :, :])
            p.dma("gpsimd", dbg_m[0:1, 0:64], mcell[0:1, 0:64])
            mo = sb["_moe"]
            for est in range(NEST):
                p.dma("gpsimd", dbg_comb[est, :, :], mo["comb_sm"][:, est * T:(est + 1) * T])
            for ci in range(NC):
                p.dma("gpsimd", dbg_slot[ci, :, 0:ES], mo["slot_b"][:, ci * ES:(ci + 1) * ES])
                p.dma("gpsimd", dbg_sob[ci, :, 0:ES], mo["slot_ob"][:, ci * ES:(ci + 1) * ES])
        p.do("gpsimd", lambda e: e.nop(), inc="g")
        p.finalize(block)
    return nc


def run_l1(x_full, w_qkv, w_out, ln_gamma, ln_beta, mu, mscale, w1, b1, w2, b2, cfg=None):
    cfg = cfg or CFG
    B, S, D = x_full.shape
    T = S // 2
    NC = D // 128
    ME, MS, MH = cfg["ME"], cfg["MS"], cfg["MH"]
    NHT = max(1, MH // 128)
    HHW = min(128, MH)
    consts = host_constants(cfg)
    COS, SIN, SH = rope_tables(cfg, cfg["DH1"], np.arange(S))
    b1h = np.zeros((HHW, ME * NHT), np.float32)
    for e in range(ME):
        for ht in range(NHT):
            b1h[:, e * NHT + ht] = b1[e, ht * 128:ht * 128 + HHW]
    b2h = np.zeros((128, ME * NC), np.float32)
    for e in range(ME):
        for ci in range(NC):
            b2h[:, e * NC + ci] = b2[e, ci * 128:(ci + 1) * 128]
    common = dict(
        wqkv=np.ascontiguousarray(w_qkv.astype(np.float32)),
        wout=np.ascontiguousarray(w_out.astype(np.float32)),
        cosd=COS, sind=SIN, shiftp_w=SH, dmask_w=consts["dmask"],
        m_gamma=np.ascontiguousarray(ln_gamma.reshape(NC, 128).T.astype(np.float32)),
        m_beta=np.ascontiguousarray(ln_beta.reshape(NC, 128).T.astype(np.float32)),
        m_mu=np.ascontiguousarray(mu.reshape(D, ME * MS).astype(np.float32)),
        m_scale=np.asarray(mscale, np.float32).reshape(1, 1),
        m_w1=np.ascontiguousarray(w1.astype(np.float32)),
        m_b1=b1h, m_w2=np.ascontiguousarray(w2.astype(np.float32)), m_b2=b2h,
        c_ones128=consts["ones128"], c_onesk1=consts["onesk1"],
        c_elow=consts["elow"], c_ehigh=consts["ehigh"], c_id128w=consts["id128"],
        c_ones64x2=consts["ones64x2"],
    )
    out = np.zeros((B, S, D), np.float32)
    for h in (0, 1):
        nc = build_l1(cfg, h)
        cores = [2 * b + h for b in range(B)]
        in_maps = [dict(xb=np.ascontiguousarray(x_full[b].T.astype(np.float32)), **common)
                   for b in range(B)]
        res = run_bass_kernel_spmd(nc, in_maps, core_ids=cores)
        for i, b in enumerate(range(B)):
            out[b, h * T:(h + 1) * T, :] = res.results[i]["xout"].T
    return out


def build_l3(cfg, h):
    D, S = cfg["D"], cfg["S"]
    NC = D // 128
    T = S // 2
    NH, DH, NKV = cfg["NH3"], cfg["DH3"], cfg["NKV3"]
    GRP = NH // NKV
    HID = cfg["HID"]
    NHG = 3
    HGS = HID // NHG
    NHC = HGS // 128
    EW = min(512, D)
    NET = EW // 128
    scale = float(np.float32(DH) ** -0.5)
    HT = h * T
    QKV_W = (NH + 2 * NKV) * DH

    nc = bass.Bass("TRN2", target_bir_lowering=False)
    stack = contextlib.ExitStack()
    xb = nc.dram_tensor("xb", [D, S], F32, kind="ExternalInput").ap()
    wqkv = nc.dram_tensor("wqkv", [D, QKV_W], F32, kind="ExternalInput").ap()
    wout = nc.dram_tensor("wout", [D, D], F32, kind="ExternalInput").ap()
    cosd = nc.dram_tensor("cosd", [128, S], F32, kind="ExternalInput").ap()
    sind = nc.dram_tensor("sind", [128, S], F32, kind="ExternalInput").ap()
    shiftp_d = nc.dram_tensor("shiftp_w", [128, 128], BF16, kind="ExternalInput").ap()
    dmask_d = nc.dram_tensor("dmask_w", [4, 128, 512], BF16, kind="ExternalInput").ap()
    wgT = nc.dram_tensor("wgT", [D, HID], I32, kind="ExternalInput").ap()
    wuT = nc.dram_tensor("wuT", [D, HID], I32, kind="ExternalInput").ap()
    wdT = nc.dram_tensor("wdT", [HID, D], I32, kind="ExternalInput").ap()
    ones128_d = nc.dram_tensor("c_ones128", [128, 1], F32, kind="ExternalInput").ap()
    onesk1_d = nc.dram_tensor("c_onesk1", [1, 128], F32, kind="ExternalInput").ap()
    xout = nc.dram_tensor("xout", [D, T], F32, kind="ExternalOutput").ap()
    kdram = nc.dram_tensor("kdram", [NKV, 128, S], BF16)
    vdram = nc.dram_tensor("vdram", [S // 128, 128, NKV * DH], BF16)
    qdram = nc.dram_tensor("qdram", [NH, 128, T], BF16)
    hn_dram = nc.dram_tensor("hn_dram", [2, NH + NKV], F32)

    sb = {}
    with stack:
        block = stack.enter_context(nc.Block())
        ps = [stack.enter_context(nc.psum_tensor(f"ps{i}", [128, 512], F32)) for i in range(8)]

        def sbuf(name, shape, dt):
            t = stack.enter_context(nc.sbuf_tensor(name, shape, dt))
            sb[name] = t
            return t

        sbuf("ones128", [128, 1], F32)
        sbuf("ones128b", [128, 1], BF16)
        sbuf("onesk1", [1, 128], F32)
        shiftp = sbuf("shiftp", [128, 128], BF16)
        evac = sbuf("evac", [128, 512], BF16)
        evac2 = sbuf("evac2", [128, 512], BF16)
        sqs = sbuf("sqs", [128, 512], F32)
        sqs2 = sbuf("sqs2", [128, 512], F32)
        mcell = sbuf("mcell", [1, 64], F32)
        negmh = sbuf("negmh", [128, NH], F32)
        s0_sb = sbuf("s0", [128, NC * T], BF16)
        g_sb = sbuf("g", [128, NC * T], BF16)
        TH = T // 2
        A1N = max(NC * S, 2 * S + T + 3072)
        ar1 = sbuf("ar1", [128, A1N], BF16)
        ar1f = ar1[:, :].bitcast(F32)
        A2F = 18432 if D == 1024 else 5440
        ar2 = sbuf("ar2", [128, 2 * A2F], BF16)
        ar2f = ar2[:, :].bitcast(F32)
        A1 = Arena(ar1[:, :], ar1f, A1N)
        A2 = Arena(ar2[:, :], ar2f, 2 * A2F)

        p = Prog(nc, stack)
        p.dma("sync", sb["ones128"][:, :], ones128_d)
        p.dma("gpsimd", sb["ones128b"][:, :], ones128_d)
        p.dma("sync", sb["onesk1"][:, :], onesk1_d)
        p.dma("sync", shiftp[:, :], shiftp_d)

        xn = A1.take_b(NC * S)
        A2.reset()
        sb["lnx"] = A2.take_f(NC * 512)
        sb["lnsq"] = A2.take_f(NC * 512)
        sb["lnrow"] = A2.take_f(2048)[0:1, :]
        emit_ln(p, sb, ps, xb, xn, S)

        A2.reset()
        wblk = A2.take_b(NC * EW)
        cos_sb = A2.take_f(S)
        sin_sb = A2.take_f(S)
        nhx = A2.take_f(2 * (8 * (S // 512) + 2))
        p.dma("sync", cos_sb[:, :], cosd)
        p.dma("sync", sin_sb[:, :], sind)

        def proj3(kind, e0, ew, out_dram, tok0, tok_n, rope, hn_slot, hn_off):
            net = ew // 128
            nech = max(1, (ew if kind != "fm" else ew) // ew)
            ntile = tok_n // 512
            for ec in range(1):
                for ci in range(NC):
                    p.dma("gpsimd", wblk[:, ci * ew:(ci + 1) * ew],
                          wqkv[ci * 128:(ci + 1) * 128, e0: e0 + ew])
                if kind == "fm":
                    for et in range(net):
                        och = et
                        for qt in range(ntile):
                            def mams(e, et=et, qt=qt):
                                r = None
                                for ci in range(NC):
                                    r = e.matmul(
                                        ps[4][:, 0:512],
                                        wblk[:, ci * ew + et * 128: ci * ew + (et + 1) * 128],
                                        xn[:, ci * S + tok0 + qt * 512:ci * S + tok0 + (qt + 1) * 512],
                                        start=(ci == 0), stop=(ci == NC - 1))
                                return r
                            p.do("tensor", mams, inc="t")
                            p.do("scalar", lambda e: e.activation(
                                out=sqs[:, 0:512], in_=ps[4][:, 0:512], func=AF.Square), inc="a")
                            p.do("tensor", lambda e: e.matmul(
                                ps[5][0:1, 0:512], sb["ones128"][:, 0:1], sqs[:, 0:512],
                                start=True, stop=True), inc="t")
                            p.do("vector", lambda e, och=och, qt=qt, ntile=ntile: e.reduce_max(
                                nhx[0:1, och * ntile + qt:och * ntile + qt + 1],
                                ps[5][0:1, 0:512], axis=AX.X, opt_input=False, opt_output=False), inc="v")
                            p.do("scalar", lambda e: e.activation(
                                out=evac[:, 0:512], in_=ps[4][:, 0:512], func=AF.Copy), inc="a")
                            if rope:
                                p.do("tensor", lambda e: e.matmul(
                                    ps[6][:, 0:512], shiftp[:, :], evac[:, 0:512],
                                    start=True, stop=True), inc="t")
                                cslice = slice(tok0 + qt * 512, tok0 + (qt + 1) * 512)
                                def ropemix(e, cslice=cslice):
                                    e.tensor_tensor(sqs[:, 0:512], evac[:, 0:512],
                                                    cos_sb[:, cslice], op=ALU.mult)
                                    e.tensor_tensor(sqs2[:, 0:512], ps[6][:, 0:512],
                                                    sin_sb[:, cslice], op=ALU.mult)
                                    return e.tensor_tensor(evac2[:, 0:512], sqs[:, 0:512],
                                                           sqs2[:, 0:512], op=ALU.add)
                                p.do("vector", ropemix, inc="v")
                                out_sb = evac2
                            else:
                                out_sb = evac
                            p.dma("sync", out_dram[och, :, qt * 512:(qt + 1) * 512], out_sb[:, 0:512])
                        NFX = 8 * (S // 512)
                        def foldmax(e, och=och, ntile=ntile, NFX=NFX):
                            return e.reduce_max(nhx[0:1, NFX:NFX + 1],
                                                nhx[0:1, och * ntile:och * ntile + ntile],
                                                axis=AX.X, opt_input=False, opt_output=False)
                        p.do("vector", foldmax, inc="v")
                        p.dma("sync", hn_dram[hn_slot, hn_off + och:hn_off + och + 1],
                              nhx[0:1, NFX:NFX + 1])
                else:
                    for tt in range(S // 128):
                        def mams(e, tt=tt):
                            r = None
                            for ci in range(NC):
                                r = e.matmul(
                                    ps[4][:, 0:ew],
                                    xn[:, ci * S + tt * 128:ci * S + (tt + 1) * 128],
                                    wblk[:, ci * ew:(ci + 1) * ew],
                                    start=(ci == 0), stop=(ci == NC - 1))
                            return r
                        p.do("tensor", mams, inc="t")
                        p.do("scalar", lambda e: e.activation(
                            out=evac[:, 0:ew], in_=ps[4][:, 0:ew], func=AF.Copy), inc="a")
                        p.dma("sync", out_dram[tt, :, 0:ew], evac[:, 0:ew])

        proj3("fm", NH * DH, NKV * DH, kdram, 0, S, True, 0, 0)
        nqe = NH * DH // EW
        for qe in range(nqe):
            proj3("fm", qe * EW, EW, qdram[qe * NET:(qe + 1) * NET], HT, T, True, 1, qe * NET)
        proj3("tm", (NH + NKV) * DH, NKV * DH, vdram, 0, S, False, None, 0)

        mkcol = sbuf("mkcol", [128, 2], F32)
        gqa_d = nc.dram_tensor("c_gqaexp", [128, NH], F32, kind="ExternalInput").ap()
        gqaexp = sbuf("gqaexp", [128, NH], F32)
        p.dma("sync", gqaexp[:, :], gqa_d)
        p.do("vector", lambda e: e.memset(mkcol[:, :], 0.0), inc="v")
        p.dma("sync", mkcol[0:NKV, 0:1], hn_dram[0, 0:NKV])
        p.dma("sync", mcell[0:1, 16:16 + NH], hn_dram[1, 0:NH])
        p.do("tensor", lambda e: e.matmul(ps[5][0:1, 0:NH], mkcol[:, 0:1],
                                          gqaexp[:, :], start=True, stop=True), inc="t")
        p.do("scalar", lambda e: e.activation(out=mcell[0:1, 0:NH], in_=ps[5][0:1, 0:NH],
                                              func=AF.Copy), inc="a")
        p.do("vector", lambda e: e.tensor_tensor(mcell[0:1, 32:32 + NH], mcell[0:1, 16:16 + NH],
                                                 mcell[0:1, 0:NH], op=ALU.mult), inc="v")
        p.do("scalar", lambda e: e.activation(out=mcell[0:1, 32:32 + NH], in_=mcell[0:1, 32:32 + NH],
                                              func=AF.Sqrt), inc="a")
        p.do("vector", lambda e: e.tensor_scalar(mcell[0:1, 48:48 + NH], mcell[0:1, 32:32 + NH],
                                                 -scale, None, op0=ALU.mult), inc="v")
        p.do("tensor", lambda e: e.matmul(ps[5][:, 0:NH], sb["onesk1"][0:1, :],
                                          mcell[0:1, 48:48 + NH], start=True, stop=True), inc="t")
        p.do("scalar", lambda e: e.activation(out=negmh[:, :], in_=ps[5][:, 0:NH], func=AF.Copy), inc="a")

        A2.reset()
        wg_sb = A2.take_b(NC * HGS)
        wu_sb = A2.take_b(NC * HGS)
        wd_sb = A2.take_b(NHC * D)
        h_sb = A2.take_b(NHC * 512)
        y_sb = A2.take_f(NC * 512)
        silu_sb = sqs2
        for tq in range(T // 512):
            for hg in range(NHG):
                for ci in range(NC):
                    p.dma("gpsimd", wg_sb[:, ci * HGS:(ci + 1) * HGS],
                          wgT[ci * 128:(ci + 1) * 128, hg * HGS:(hg + 1) * HGS])
                    p.dma("gpsimd", wu_sb[:, ci * HGS:(ci + 1) * HGS],
                          wuT[ci * 128:(ci + 1) * 128, hg * HGS:(hg + 1) * HGS])
                for hc in range(NHC):
                    p.dma("gpsimd", wd_sb[:, hc * D:(hc + 1) * D],
                          wdT[hg * HGS + hc * 128: hg * HGS + (hc + 1) * 128, :])
                for hc in range(NHC):
                    def gatemm(e, tq=tq, hc=hc):
                        r = None
                        for ci in range(NC):
                            e.matmul(ps[4][:, 0:512],
                                     wg_sb[:, ci * HGS + hc * 128: ci * HGS + (hc + 1) * 128],
                                     xn[:, ci * S + HT + tq * 512: ci * S + HT + (tq + 1) * 512],
                                     start=(ci == 0), stop=(ci == NC - 1))
                            r = e.matmul(ps[5][:, 0:512],
                                         wu_sb[:, ci * HGS + hc * 128: ci * HGS + (hc + 1) * 128],
                                         xn[:, ci * S + HT + tq * 512: ci * S + HT + (tq + 1) * 512],
                                         start=(ci == 0), stop=(ci == NC - 1))
                        return r
                    p.do("tensor", gatemm, inc="t")
                    p.do("scalar", lambda e: e.activation(out=silu_sb[:, 0:512], in_=ps[4][:, 0:512],
                                                          func=AF.Silu, scale=float(DEQ)), inc="a")
                    p.do("vector", lambda e, hc=hc: e.scalar_tensor_tensor(
                        h_sb[:, hc * 512:(hc + 1) * 512], ps[5][:, 0:512], float(DEQ),
                        silu_sb[:, 0:512], op0=ALU.mult, op1=ALU.mult), inc="v")
                for dc in range(NC):
                    def dmm(e, dc=dc):
                        r = None
                        for hc in range(NHC):
                            r = e.matmul(ps[6 + dc % 2][:, 0:512],
                                         wd_sb[:, hc * D + dc * 128: hc * D + (dc + 1) * 128],
                                         h_sb[:, hc * 512:(hc + 1) * 512],
                                         start=(hc == 0), stop=(hc == NHC - 1))
                        return r
                    p.do("tensor", dmm, inc="t")
                    def yadd(e, dc=dc, hg=hg):
                        dst = y_sb[:, dc * 512:(dc + 1) * 512]
                        if hg == 0:
                            return e.tensor_copy(dst, ps[6 + dc % 2][:, 0:512])
                        return e.tensor_tensor(dst, dst, ps[6 + dc % 2][:, 0:512], op=ALU.add)
                    p.do("vector", yadd, inc="v")
            def gsilu(e, tq=tq):
                r = None
                for dc in range(NC):
                    r = e.activation(out=g_sb[:, dc * T + tq * 512: dc * T + (tq + 1) * 512],
                                     in_=y_sb[:, dc * 512:(dc + 1) * 512],
                                     func=AF.Silu, scale=float(DEQ))
                return r
            p.do("scalar", gsilu, inc="a")

        A1.reset()
        khp = A1.take_b(S)
        vhp = A1.take_b(S)
        qhp = A1.take_b(T)
        ptA = A1.take_b(512)
        rb = A1.take_f(512)
        A2.reset()
        dmask = A2.take_b(4 * 512)
        rrow = A2.take_f(1024)[0:1, :]
        for r in range(4):
            p.dma("sync", dmask[:, r * 512:(r + 1) * 512], dmask_d[r, :, :])

        for hh in range(NH):
            kv = hh // GRP
            p.dma("sync", khp[:, :], kdram[kv, :, :])
            p.dma("sync", qhp[:, :], qdram[hh, :, :])
            for tt in range(S // 128):
                p.dma("sync", vhp[:, tt * 128:(tt + 1) * 128],
                      vdram[tt, :, kv * DH:(kv + 1) * DH])
            for qt in range(T // 512):
                qglob = HT + qt * 512
                ext = (qglob + 512) // 128
                for kc in range(ext):
                    diag = kc * 128 >= qglob
                    rmask = (kc * 128 - qglob) // 128 if diag else None
                    p.do("tensor", lambda e, kc=kc, qt=qt: e.matmul(
                        ps[4][:, 0:512],
                        khp[:, kc * 128:(kc + 1) * 128],
                        qhp[:, qt * 512:(qt + 1) * 512],
                        start=True, stop=True), inc="t")
                    p.do("scalar", lambda e, hh=hh: e.activation(
                        out=ptA[:, 0:512], in_=ps[4][:, 0:512],
                        func=AF.Exp, bias=negmh[:, hh:hh + 1], scale=scale), inc="a")
                    if diag:
                        p.do("vector", lambda e, rmask=rmask: e.tensor_tensor(
                            ptA[:, 0:512], ptA[:, 0:512],
                            dmask[:, rmask * 512:(rmask + 1) * 512], op=ALU.mult), inc="v")
                    def avmm(e, kc=kc, ext=ext):
                        e.matmul(ps[0][:, 0:512], vhp[:, kc * 128:(kc + 1) * 128],
                                 ptA[:, 0:512], start=(kc == 0), stop=(kc == ext - 1))
                        return e.matmul(ps[2][0:1, 0:512], sb["ones128b"][:, 0:1], ptA[:, 0:512],
                                        start=(kc == 0), stop=(kc == ext - 1))
                    p.do("tensor", avmm, inc="t")
                p.do("vector", lambda e: e.reciprocal(rrow[0:1, 0:512], ps[2][0:1, 0:512]), inc="v")
                p.do("tensor", lambda e: e.matmul(ps[6][:, 0:512], sb["onesk1"][0:1, :],
                                                  rrow[0:1, 0:512], start=True, stop=True), inc="t")
                p.do("scalar", lambda e: e.activation(out=rb[:, 0:512], in_=ps[6][:, 0:512],
                                                      func=AF.Copy), inc="a")
                p.do("vector", lambda e, hh=hh, qt=qt: e.tensor_tensor(
                    s0_sb[:, hh * T + qt * 512:hh * T + (qt + 1) * 512],
                    ps[0][:, 0:512], rb[:, 0:512], op=ALU.mult), inc="v")

        A1.reset()
        s_sb = A1.take_b(NC * T)
        A2.reset()
        wo_sb = A2.take_b(NC * 128)
        for ot in range(NC):
            for ci in range(NC):
                p.dma("gpsimd", wo_sb[:, ci * 128:(ci + 1) * 128],
                      wout[ci * 128:(ci + 1) * 128, ot * 128:(ot + 1) * 128])
            for qt in range(T // 512):
                def omm(e, ot=ot, qt=qt):
                    r = None
                    for ci in range(NC):
                        r = e.matmul(ps[4][:, 0:512],
                                     wo_sb[:, ci * 128:(ci + 1) * 128],
                                     s0_sb[:, ci * T + qt * 512:ci * T + (qt + 1) * 512],
                                     start=(ci == 0), stop=(ci == NC - 1))
                    return r
                p.do("tensor", omm, inc="t")
                p.do("scalar", lambda e, ot=ot, qt=qt: e.activation(
                    out=s_sb[:, ot * T + qt * 512:ot * T + (qt + 1) * 512],
                    in_=ps[4][:, 0:512], func=AF.Copy), inc="a")

        A2.reset()
        xres = A2.take_f(512)
        for tq in range(T // 512):
            for ci in range(NC):
                p.dma("sync", xres[:, 0:512],
                      xb[ci * 128:(ci + 1) * 128, HT + tq * 512:HT + (tq + 1) * 512])
                def resid(e, tq=tq, ci=ci):
                    cs = slice(ci * T + tq * 512, ci * T + (tq + 1) * 512)
                    e.tensor_tensor(sqs[:, 0:512], s_sb[:, cs], g_sb[:, cs], op=ALU.mult)
                    return e.tensor_tensor(xres[:, 0:512], xres[:, 0:512], sqs[:, 0:512], op=ALU.add)
                p.do("vector", resid, inc="v")
                p.dma("scalar", xout[ci * 128:(ci + 1) * 128, tq * 512:(tq + 1) * 512],
                      xres[:, 0:512])

        p.do("gpsimd", lambda e: e.nop(), inc="g")
        p.finalize(block)
    return nc


def run_l3(x_full, Wq, Wk, Wv, Wo, wg, wu, wd, cfg=None):
    cfg = cfg or CFG
    B, S, D = x_full.shape
    T = S // 2
    consts = host_constants(cfg)
    COS, SIN, SH = rope_tables(cfg, cfg["DH3"], np.arange(S))
    wqkv = np.concatenate([Wq, Wk, Wv], axis=1).astype(np.float32)
    NH, NKV = cfg["NH3"], cfg["NKV3"]
    GRP = NH // NKV
    gqa = np.zeros((128, NH), np.float32)
    for hh in range(NH):
        gqa[hh // GRP, hh] = 1.0
    common = dict(
        wqkv=np.ascontiguousarray(wqkv), wout=np.ascontiguousarray(Wo.astype(np.float32)),
        cosd=COS, sind=SIN, shiftp_w=SH, dmask_w=consts["dmask"],
        wgT=np.ascontiguousarray(wg.T), wuT=np.ascontiguousarray(wu.T),
        wdT=np.ascontiguousarray(wd.T), c_gqaexp=gqa,
        c_ones128=consts["ones128"], c_onesk1=consts["onesk1"],
    )
    out = np.zeros((B, S, D), np.float32)
    for h in (0, 1):
        nc = build_l3(cfg, h)
        cores = [2 * b + h for b in range(B)]
        in_maps = [dict(xb=np.ascontiguousarray(x_full[b].T.astype(np.float32)), **common)
                   for b in range(B)]
        res = run_bass_kernel_spmd(nc, in_maps, core_ids=cores)
        for i, b in enumerate(range(B)):
            out[b, h * T:(h + 1) * T, :] = res.results[i]["xout"].T
    return out


def build_l2a(cfg, h):
    D, S = cfg["D"], cfg["S"]
    NC = D // 128
    T = S // 2
    NH, DH = cfg["NH1"], cfg["DH1"]
    NHP = NH // 2
    E2 = cfg["E2"]
    EW = min(512, D)
    NET = EW // 128
    NECH = max(1, D // EW)
    scale = float(np.float32(DH) ** -0.5)
    HT = h * T

    nc = bass.Bass("TRN2", target_bir_lowering=False)
    stack = contextlib.ExitStack()
    xb = nc.dram_tensor("xb", [D, S], F32, kind="ExternalInput").ap()
    wqkv = nc.dram_tensor("wqkv", [D, 3 * D], F32, kind="ExternalInput").ap()
    wout = nc.dram_tensor("wout", [D, D], F32, kind="ExternalInput").ap()
    wsw = nc.dram_tensor("wsw", [D, E2], F32, kind="ExternalInput").ap()
    bsw = nc.dram_tensor("bsw", [1, E2], F32, kind="ExternalInput").ap()
    dmask_d = nc.dram_tensor("dmask_w", [4, 128, 512], BF16, kind="ExternalInput").ap()
    iota128_d = nc.dram_tensor("c_iota128", [128, 2], F32, kind="ExternalInput").ap()
    ones128_d = nc.dram_tensor("c_ones128", [128, 1], F32, kind="ExternalInput").ap()
    onesk1_d = nc.dram_tensor("c_onesk1", [1, 128], F32, kind="ExternalInput").ap()
    elow_d = nc.dram_tensor("c_elow", [1, 128], F32, kind="ExternalInput").ap()
    ehigh_d = nc.dram_tensor("c_ehigh", [1, 128], F32, kind="ExternalInput").ap()
    s_out = nc.dram_tensor("s_out", [NC, 128, T], BF16, kind="ExternalOutput").ap()
    xn_out = nc.dram_tensor("xn_out", [NC, 128, T], BF16, kind="ExternalOutput").ap()
    xnt_out = nc.dram_tensor("xnt_out", [T, D], BF16, kind="ExternalOutput").ap()
    routes_out = nc.dram_tensor("routes_out", [1, T], F32, kind="ExternalOutput").ap()
    lpos_out = nc.dram_tensor("lpos_out", [1, T], F32, kind="ExternalOutput").ap()
    counts_out = nc.dram_tensor("counts_out", [E2, 1], F32, kind="ExternalOutput").ap()
    kdram = nc.dram_tensor("kdram", [NC, 128, S], BF16)
    vdram = nc.dram_tensor("vdram", [S // 128, 128, D], BF16)
    qdram = nc.dram_tensor("qdram", [NC, 128, T], BF16)
    hn_dram = nc.dram_tensor("hn_dram", [2, NH], F32)
    rts_dram = nc.dram_tensor("rts_dram", [1, T], F32)

    sb = {}
    with stack:
        block = stack.enter_context(nc.Block())
        ps = [stack.enter_context(nc.psum_tensor(f"ps{i}", [128, 512], F32)) for i in range(8)]

        def sbuf(name, shape, dt):
            t = stack.enter_context(nc.sbuf_tensor(name, shape, dt))
            sb[name] = t
            return t

        sbuf("ones128", [128, 1], F32)
        sbuf("ones128b", [128, 1], BF16)
        sbuf("onesk1", [1, 128], F32)
        sbuf("elow", [1, 128], F32)
        sbuf("ehigh", [1, 128], F32)
        evac = sbuf("evac", [128, 512], BF16)
        sqs = sbuf("sqs", [128, 512], F32)
        mcell = sbuf("mcell", [1, 64], F32)
        negmh = sbuf("negmh", [128, NH], F32)
        iota128 = sbuf("iota128", [128, 2], F32)
        s0_sb = sbuf("s0", [128, NC * T], BF16)
        TH = T // 2
        A1N = max(NC * S, 2 * S + T + 3072)
        ar1 = sbuf("ar1", [128, A1N], BF16)
        ar1f = ar1[:, :].bitcast(F32)
        A2F = 16000 if D == 1024 else 5440
        ar2 = sbuf("ar2", [128, 2 * A2F], BF16)
        ar2f = ar2[:, :].bitcast(F32)
        A1 = Arena(ar1[:, :], ar1f, A1N)
        A2 = Arena(ar2[:, :], ar2f, 2 * A2F)

        p = Prog(nc, stack)
        p.dma("sync", sb["ones128"][:, :], ones128_d)
        p.dma("gpsimd", sb["ones128b"][:, :], ones128_d)
        p.dma("sync", sb["onesk1"][:, :], onesk1_d)
        p.dma("sync", sb["elow"][:, :], elow_d)
        p.dma("sync", sb["ehigh"][:, :], ehigh_d)
        p.dma("sync", iota128[:, :], iota128_d)

        xn = A1.take_b(NC * S)
        A2.reset()
        sb["lnx"] = A2.take_f(NC * 512)
        sb["lnsq"] = A2.take_f(NC * 512)
        sb["lnrow"] = A2.take_f(2048)[0:1, :]
        emit_ln(p, sb, ps, xb, xn, S)

        for ci in range(NC):
            p.dma("sync", xn_out[ci, :, :], xn[:, ci * S + HT:ci * S + HT + T])
        A2.reset()
        xtq = A2.take_b((T // 4 // 128) * D)
        for q4 in range(4):
            SQ4 = T // 4
            for ci in range(NC):
                xt = xtq.rearrange("p (a b) -> p a b", b=D)[:, :, ci * 128:(ci + 1) * 128]
                p.dma("scalar", xt, xn[:, ci * S + HT + q4 * SQ4:ci * S + HT + (q4 + 1) * SQ4],
                      transpose=True, sem="d_sc")
            p.dma("sync", xnt_out[q4 * SQ4:(q4 + 1) * SQ4, :].rearrange("(a p) d -> p a d", p=128),
                  xtq.rearrange("p (a b) -> p a b", b=D))

        wsw_sb = A2.take_b(NC * E2)
        bswbc = A2.take_f(E2)
        rowsc = A2.take_f(64)
        for ci in range(NC):
            p.dma("gpsimd", wsw_sb[:, ci * E2:(ci + 1) * E2],
                  wsw[ci * 128:(ci + 1) * 128, :])
        p.dma("sync", rowsc[0:1, 0:E2], bsw)
        p.do("tensor", lambda e: e.matmul(ps[5][:, 0:E2], sb["onesk1"][0:1, :], rowsc[0:1, 0:E2],
                                          start=True, stop=True), inc="t")
        p.do("scalar", lambda e: e.activation(out=bswbc[:, 0:E2], in_=ps[5][:, 0:E2],
                                              func=AF.Copy), inc="a")
        lg8 = A2.take_f(2 * E2 + 8)
        mx8 = A2.take_f(16)
        rcol = A2.take_f(4)
        iota8bc = A2.take_f(E2 + 2)
        iota8d = nc.dram_tensor("c_iota8bc", [128, E2], F32, kind="ExternalInput").ap()
        p.dma("sync", iota8bc[:, 0:E2], iota8d)
        for tt in range(T // 128):
            def rmm(e, tt=tt):
                r = None
                for ci in range(NC):
                    r = e.matmul(ps[4][:, 0:E2],
                                 xn[:, ci * S + HT + tt * 128:ci * S + HT + (tt + 1) * 128],
                                 wsw_sb[:, ci * E2:(ci + 1) * E2],
                                 start=(ci == 0), stop=(ci == NC - 1))
                return r
            p.do("tensor", rmm, inc="t")
            p.do("vector", lambda e: e.tensor_tensor(lg8[:, 0:E2], ps[4][:, 0:E2],
                                                     bswbc[:, 0:E2], op=ALU.add), inc="v")
            p.do("vector", lambda e: e.reduce_max(mx8[:, 0:1], lg8[:, 0:E2], axis=AX.X,
                                                  opt_input=False, opt_output=False), inc="v")
            p.do("vector", lambda e: e.tensor_scalar(lg8[:, 8:8 + E2], lg8[:, 0:E2],
                                                     mx8[:, 0:1], None, op0=ALU.is_ge), inc="v")
            p.do("vector", lambda e: e.tensor_tensor(lg8[:, 8:8 + E2], lg8[:, 8:8 + E2],
                                                     iota8bc[:, 0:E2], op=ALU.mult), inc="v")
            p.do("vector", lambda e: e.reduce_sum(mx8[:, 2:3], lg8[:, 8:8 + E2], axis=AX.X,
                                                  opt_input=False, opt_output=False), inc="v")
            p.do("scalar", lambda e: e.activation(out=rcol[:, 0:1], in_=mx8[:, 2:3],
                                                  func=AF.Copy), inc="a")
            p.dma("sync", rts_dram[0, tt * 128:(tt + 1) * 128], rcol[:, 0:1])
        rrow_ = A2.take_f(T)
        ohp = A2.take_f(T)
        p.dma("sync", rrow_[0:1, 0:T], rts_dram[0, :])
        for qt in range(T // 512):
            p.do("tensor", lambda e, qt=qt: e.matmul(
                ps[4][:, 0:512], sb["onesk1"][0:1, :], rrow_[0:1, qt * 512:(qt + 1) * 512],
                start=True, stop=True), inc="t")
            p.do("scalar", lambda e, qt=qt: e.activation(
                out=ohp[:, qt * 512:(qt + 1) * 512], in_=ps[4][:, 0:512], func=AF.Copy), inc="a")
        p.do("vector", lambda e: e.tensor_scalar(ohp[:, :], ohp[:, :], iota128[:, 0:1], None,
                                                 op0=ALU.is_equal), inc="v")
        p.dma("sync", routes_out[0, :], rrow_[0:1, 0:T])
        posi = A2.take_f(T)
        p.do("vector", lambda e: e.tensor_tensor_scan(posi[:, :], ohp[:, :], ohp[:, :],
                                                      initial=0.0, op0=ALU.add, op1=ALU.bypass), inc="v")
        p.dma("sync", counts_out[0:E2, 0:1], posi[0:E2, T - 1:T])
        p.do("vector", lambda e: e.tensor_tensor(ohp[:, :], ohp[:, :], posi[:, :], op=ALU.mult), inc="v")
        for qt in range(T // 512):
            p.do("tensor", lambda e, qt=qt: e.matmul(
                ps[4][0:1, 0:512], sb["ones128"][:, 0:1], ohp[:, qt * 512:(qt + 1) * 512],
                start=True, stop=True), inc="t")
            p.do("vector", lambda e, qt=qt: e.tensor_scalar(
                sqs[0:1, 0:512], ps[4][0:1, 0:512], -1.0, None, op0=ALU.add), inc="v")
            p.dma("sync", lpos_out[0, qt * 512:(qt + 1) * 512], sqs[0:1, 0:512])

        A2.reset()
        wblk = A2.take_b(NC * EW)
        nhx = A2.take_f(2 * (8 * (S // 512) + 2))

        def proj2(kind, e0, out_dram, tok0, tok_n, hn_slot):
            ntile = tok_n // 512
            for ec in range(NECH):
                for ci in range(NC):
                    p.dma("gpsimd", wblk[:, ci * EW:(ci + 1) * EW],
                          wqkv[ci * 128:(ci + 1) * 128, e0 + ec * EW: e0 + (ec + 1) * EW])
                if kind == "fm":
                    for et in range(NET):
                        och = ec * NET + et
                        for qt in range(ntile):
                            def mams(e, ec=ec, et=et, qt=qt):
                                r = None
                                for ci in range(NC):
                                    r = e.matmul(
                                        ps[4][:, 0:512],
                                        wblk[:, ci * EW + et * 128: ci * EW + (et + 1) * 128],
                                        xn[:, ci * S + tok0 + qt * 512:ci * S + tok0 + (qt + 1) * 512],
                                        start=(ci == 0), stop=(ci == NC - 1))
                                return r
                            p.do("tensor", mams, inc="t")
                            p.do("scalar", lambda e: e.activation(
                                out=sqs[:, 0:512], in_=ps[4][:, 0:512], func=AF.Square), inc="a")
                            p.do("tensor", lambda e: e.matmul(
                                ps[5][0:2, 0:512], sb["ones64x2"][:, 0:2], sqs[:, 0:512],
                                start=True, stop=True), inc="t")
                            p.do("vector", lambda e, och=och, qt=qt, ntile=ntile: e.reduce_max(
                                nhx[0:2, och * ntile + qt:och * ntile + qt + 1],
                                ps[5][0:2, 0:512], axis=AX.X, opt_input=False, opt_output=False), inc="v")
                            p.do("scalar", lambda e: e.activation(
                                out=evac[:, 0:512], in_=ps[4][:, 0:512], func=AF.Copy), inc="a")
                            p.dma("sync", out_dram[och, :, qt * 512:(qt + 1) * 512], evac[:, 0:512])
                        NFX = 8 * (S // 512)
                        def foldmax(e, och=och, ntile=ntile, NFX=NFX):
                            return e.reduce_max(nhx[0:2, NFX:NFX + 1],
                                                nhx[0:2, och * ntile:och * ntile + ntile],
                                                axis=AX.X, opt_input=False, opt_output=False)
                        p.do("vector", foldmax, inc="v")
                        p.dma("sync", hn_dram[hn_slot, och * 2:och * 2 + 2],
                              nhx[0:2, NFX:NFX + 1])
                else:
                    for tt in range(S // 128):
                        def mams(e, ec=ec, tt=tt):
                            r = None
                            for ci in range(NC):
                                r = e.matmul(
                                    ps[4][:, 0:EW],
                                    xn[:, ci * S + tt * 128:ci * S + (tt + 1) * 128],
                                    wblk[:, ci * EW:(ci + 1) * EW],
                                    start=(ci == 0), stop=(ci == NC - 1))
                            return r
                        p.do("tensor", mams, inc="t")
                        p.do("scalar", lambda e: e.activation(
                            out=evac[:, 0:EW], in_=ps[4][:, 0:EW], func=AF.Copy), inc="a")
                        p.dma("sync", out_dram[tt, :, ec * EW:(ec + 1) * EW], evac[:, 0:EW])

        sbuf("ones64x2", [128, 2], F32)
        o64_d = nc.dram_tensor("c_ones64x2", [128, 2], F32, kind="ExternalInput").ap()
        p.dma("sync", sb["ones64x2"][:, :], o64_d)
        proj2("fm", D, kdram, 0, S, 0)
        proj2("fm", 0, qdram, HT, T, 1)
        proj2("tm", 2 * D, vdram, 0, S, None)

        p.dma("sync", mcell[0:1, 0:NH], hn_dram[0, :])
        p.dma("sync", mcell[0:1, NH:2 * NH], hn_dram[1, :])
        p.do("vector", lambda e: e.tensor_tensor(mcell[0:1, 0:NH], mcell[0:1, 0:NH],
                                                 mcell[0:1, NH:2 * NH], op=ALU.mult), inc="v")
        p.do("scalar", lambda e: e.activation(out=mcell[0:1, 0:NH], in_=mcell[0:1, 0:NH], func=AF.Sqrt), inc="a")
        p.do("vector", lambda e: e.tensor_scalar(mcell[0:1, NH:2 * NH], mcell[0:1, 0:NH],
                                                 -scale, None, op0=ALU.mult), inc="v")
        p.do("tensor", lambda e: e.matmul(ps[5][:, 0:NH], sb["onesk1"][0:1, :],
                                          mcell[0:1, NH:2 * NH], start=True, stop=True), inc="t")
        p.do("scalar", lambda e: e.activation(out=negmh[:, :], in_=ps[5][:, 0:NH], func=AF.Copy), inc="a")

        A1.reset()
        khp = A1.take_b(S)
        vhp = A1.take_b(S)
        qhp = A1.take_b(T)
        ptA = A1.take_b(512)
        ptB = A1.take_b(512)
        rb = A1.take_f(512)
        A2.reset()
        dmask = A2.take_b(4 * 512)
        rrow = A2.take_f(1024)[0:1, :]
        for r in range(4):
            p.dma("sync", dmask[:, r * 512:(r + 1) * 512], dmask_d[r, :, :])

        for hp in range(NHP):
            p.dma("sync", khp[:, :], kdram[hp, :, :])
            p.dma("sync", qhp[:, :], qdram[hp, :, :])
            for tt in range(S // 128):
                p.dma("sync", vhp[:, tt * 128:(tt + 1) * 128],
                      vdram[tt, :, hp * 128:(hp + 1) * 128])
            for qt in range(T // 512):
                qglob = HT + qt * 512
                ext = (qglob + 512) // 128
                for kc in range(ext):
                    diag = kc * 128 >= qglob
                    rmask = (kc * 128 - qglob) // 128 if diag else None
                    def smm(e, hp=hp, qt=qt, kc=kc):
                        e.matmul(ps[4][:, 0:512],
                                 khp[0:64, kc * 128:(kc + 1) * 128],
                                 qhp[0:64, qt * 512:(qt + 1) * 512],
                                 start=True, stop=True)
                        return e.matmul(ps[5][:, 0:512],
                                        khp[64:128, kc * 128:(kc + 1) * 128],
                                        qhp[64:128, qt * 512:(qt + 1) * 512],
                                        start=True, stop=True, tile_position=(64, 0))
                    p.do("tensor", smm, inc="t")
                    def expab(e, hp=hp, kc=kc):
                        e.activation(out=ptA[:, 0:512], in_=ps[4][:, 0:512], func=AF.Exp,
                                     bias=negmh[:, 2 * hp:2 * hp + 1], scale=scale)
                        return e.activation(out=ptB[:, 0:512], in_=ps[5][:, 0:512], func=AF.Exp,
                                            bias=negmh[:, 2 * hp + 1:2 * hp + 2], scale=scale)
                    p.do("scalar", expab, inc="a")
                    if diag:
                        def maskab(e, rmask=rmask):
                            e.tensor_tensor(ptA[:, 0:512], ptA[:, 0:512],
                                            dmask[:, rmask * 512:(rmask + 1) * 512], op=ALU.mult)
                            return e.tensor_tensor(ptB[:, 0:512], ptB[:, 0:512],
                                                   dmask[:, rmask * 512:(rmask + 1) * 512], op=ALU.mult)
                        p.do("vector", maskab, inc="v")
                    def avmm(e, kc=kc, ext=ext):
                        e.matmul(ps[0][0:64, 0:512], vhp[:, kc * 128:kc * 128 + 64],
                                 ptA[:, 0:512], start=(kc == 0), stop=(kc == ext - 1))
                        e.matmul(ps[1][64:128, 0:512], vhp[:, kc * 128 + 64:(kc + 1) * 128],
                                 ptB[:, 0:512], start=(kc == 0), stop=(kc == ext - 1),
                                 tile_position=(0, 64))
                        e.matmul(ps[2][0:1, 0:512], sb["ones128b"][:, 0:1], ptA[:, 0:512],
                                 start=(kc == 0), stop=(kc == ext - 1))
                        return e.matmul(ps[3][0:1, 0:512], sb["ones128b"][:, 0:1], ptB[:, 0:512],
                                        start=(kc == 0), stop=(kc == ext - 1))
                    p.do("tensor", avmm, inc="t")
                def recips(e):
                    e.reciprocal(rrow[0:1, 0:512], ps[2][0:1, 0:512])
                    return e.reciprocal(rrow[0:1, 512:1024], ps[3][0:1, 0:512])
                p.do("vector", recips, inc="v")
                def rbc(e):
                    e.matmul(ps[6][:, 0:512], sb["elow"][0:1, :], rrow[0:1, 0:512],
                             start=True, stop=True)
                    return e.matmul(ps[6][:, 0:512], sb["ehigh"][0:1, :], rrow[0:1, 512:1024],
                                    start=False, stop=True)
                p.do("tensor", rbc, inc="t")
                p.do("scalar", lambda e: e.activation(out=rb[:, 0:512], in_=ps[6][:, 0:512],
                                                      func=AF.Copy), inc="a")
                def s0ev(e, hp=hp, qt=qt):
                    cs = slice(hp * T + qt * 512, hp * T + (qt + 1) * 512)
                    e.tensor_tensor(s0_sb[0:64, cs], ps[0][0:64, 0:512], rb[0:64, 0:512], op=ALU.mult)
                    return e.tensor_tensor(s0_sb[64:128, cs], ps[1][64:128, 0:512],
                                           rb[64:128, 0:512], op=ALU.mult)
                p.do("vector", s0ev, inc="v")

        A2.reset()
        wo_sb = A2.take_b(NC * 128)
        for ot in range(NC):
            for ci in range(NC):
                p.dma("gpsimd", wo_sb[:, ci * 128:(ci + 1) * 128],
                      wout[ci * 128:(ci + 1) * 128, ot * 128:(ot + 1) * 128])
            for qt in range(T // 512):
                def omm(e, ot=ot, qt=qt):
                    r = None
                    for ci in range(NC):
                        r = e.matmul(ps[4][:, 0:512],
                                     wo_sb[:, ci * 128:(ci + 1) * 128],
                                     s0_sb[:, ci * T + qt * 512:ci * T + (qt + 1) * 512],
                                     start=(ci == 0), stop=(ci == NC - 1))
                    return r
                p.do("tensor", omm, inc="t")
                p.do("scalar", lambda e: e.activation(out=evac[:, 0:512], in_=ps[4][:, 0:512],
                                                      func=AF.Copy), inc="a")
                p.dma("sync", s_out[ot, :, qt * 512:(qt + 1) * 512], evac[:, 0:512])

        p.do("gpsimd", lambda e: e.nop(), inc="g")
        p.finalize(block)
    return nc


def build_l2b(cfg, upto=5):
    D, S, B = cfg["D"], cfg["S"], cfg["B"]
    NC = D // 128
    T = S // 2
    E2, FF2 = cfg["E2"], cfg["FF2"]
    NFC = FF2 // 128
    CAP = B * S // E2
    NT = T // 128 + E2 - 1
    NTT = T // 128
    DBW = min(512, D)
    NDB = D // DBW

    nc = bass.Bass("TRN2", target_bir_lowering=False)
    stack = contextlib.ExitStack()
    x2 = nc.dram_tensor("x2", [D, T], F32, kind="ExternalInput").ap()
    s2 = nc.dram_tensor("s2", [NC, 128, T], BF16, kind="ExternalInput").ap()
    xnt = nc.dram_tensor("xnt", [T, D], BF16, kind="ExternalInput").ap()
    routes_d = nc.dram_tensor("routes_d", [1, T], F32, kind="ExternalInput").ap()
    lpos_d = nc.dram_tensor("lpos_d", [1, T], F32, kind="ExternalInput").ap()
    counts_d = nc.dram_tensor("counts_d", [8, E2], F32, kind="ExternalInput").ap()
    pref_d = nc.dram_tensor("pref_d", [8, 2], F32, kind="ExternalInput").ap()
    w1_d = nc.dram_tensor("m2_w1", [E2, D, FF2], F32, kind="ExternalInput").ap()
    w2_d = nc.dram_tensor("m2_w2", [E2, FF2, D], F32, kind="ExternalInput").ap()
    b1_d = nc.dram_tensor("m2_b1", [128, E2 * NFC], F32, kind="ExternalInput").ap()
    b2_d = nc.dram_tensor("m2_b2", [1, E2 * D], F32, kind="ExternalInput").ap()
    iota128_d = nc.dram_tensor("c_iota128", [128, 2], F32, kind="ExternalInput").ap()
    tbase_d = nc.dram_tensor("c_tbase", [1, 64], F32, kind="ExternalInput").ap()
    ones128_d = nc.dram_tensor("c_ones128", [128, 1], F32, kind="ExternalInput").ap()
    onesk1_d = nc.dram_tensor("c_onesk1", [1, 128], F32, kind="ExternalInput").ap()
    xout = nc.dram_tensor("xout", [D, T], F32, kind="ExternalOutput").ap()
    dbg_rows = nc.dram_tensor("dbg_rows", [8, T], F32, kind="ExternalOutput").ap()
    dbg_sx = nc.dram_tensor("dbg_sx", [NT * 128 + 128, D], BF16, kind="ExternalOutput").ap()
    sorted_x = nc.dram_tensor("sorted_x", [NT * 128 + 128, D], BF16)
    sorted_o = nc.dram_tensor("sorted_o", [NT * 128 + 128, D], BF16)

    sb = {}
    with stack:
        block = stack.enter_context(nc.Block())
        ps = [stack.enter_context(nc.psum_tensor(f"ps{i}", [128, 512], F32)) for i in range(7)]
        psb = stack.enter_context(nc.psum_tensor("psb", [128, 1024], BF16))

        def sbuf(name, shape, dt):
            t = stack.enter_context(nc.sbuf_tensor(name, shape, dt))
            sb[name] = t
            return t

        sbuf("ones128", [128, 1], F32)
        sbuf("ones128b", [128, 1], BF16)
        sbuf("onesk1", [1, 128], F32)
        iota128 = sbuf("iota128", [128, 2], F32)
        rowbuf = sbuf("rowbuf", [1, 3 * T + 128], F32)
        xe_all = sbuf("xe_all", [128, NT * D], BF16)
        y_all = sbuf("y_all", [128, NT * D], BF16)
        HBW = min(1024, FF2)
        NHQ = HBW // 128
        htile = sbuf("htile", [128, HBW], BF16)
        ytmp = sbuf("ytmp", [128, 1024], F32)
        w1blk = sbuf("w1blk", [128, NC * HBW], BF16)
        w2blk = sbuf("w2blk", [128, NHQ * D], BF16)
        b2bc = sbuf("b2bc", [128, D], F32)
        b1s = sbuf("b1s", [128, E2 * NFC], F32)
        eidi = sbuf("eidi", [1, 64], I32)
        usei = sbuf("usei", [1, 64], I32)
        desti = sbuf("desti", [128, 2 * NTT], I32)
        keepc = sbuf("keepc", [128, 2 * NTT], F32)
        scr = sbuf("scr", [128, 512], F32)
        scr2 = sbuf("scr2", [128, 1024], BF16)
        gfm = sbuf("gfm", [128, 1024], BF16)
        ohp_sb = sbuf("ohp", [128, T], F32)
        posk = sbuf("posk", [128, T], F32)
        kbc = sbuf("kbc", [128, T], F32)
        cntm = sbuf("cntm", [128, E2], F32)
        prefc = sbuf("prefc", [128, 2], F32)
        offc = sbuf("offc", [128, 2], F32)
        gbc = sbuf("gbc", [128, 2], F32)
        icell = sbuf("icell", [1, 32], I32)
        tbase = sbuf("tbase", [1, 64], F32)
        tbb = sbuf("tbb", [128, 64], F32)
        xres = sbuf("xres", [128, 128], F32)

        p = Prog(nc, stack)
        p.dma("sync", sb["ones128"][:, :], ones128_d)
        p.dma("gpsimd", sb["ones128b"][:, :], ones128_d)
        p.dma("sync", sb["onesk1"][:, :], onesk1_d)
        p.dma("sync", iota128[:, :], iota128_d)
        p.dma("sync", b1s[:, :], b1_d)
        p.dma("sync", tbase[0:1, 0:64], tbase_d)

        p.do("vector", lambda e: e.memset(rowbuf[:, :], 0.0), inc="v")
        p.do("vector", lambda e: e.memset(scr[:, :], 0.0), inc="v")
        p.do("vector", lambda e: e.memset(ytmp[:, :], 0.0), inc="v")
        p.do("vector", lambda e: e.memset(y_all[:, :], 0.0), inc="v")
        p.do("vector", lambda e: e.memset(htile[:, :], 0.0), inc="v")
        p.do("vector", lambda e: e.memset(xe_all[:, :], 0.0), inc="v")
        p.do("vector", lambda e: e.memset(gfm[:, :], 0.0), inc="v")
        p.do("vector", lambda e: e.memset(desti[:, :], 0), inc="v")
        R_route = rowbuf[0:1, 0:T]
        R_lpos = rowbuf[0:1, T:2 * T]
        R_keep = rowbuf[0:1, 2 * T:3 * T]
        R_ptok = R_keep
        R_r = rowbuf[0:1, 3 * T:3 * T + 128]
        p.dma("sync", R_route, routes_d)
        p.dma("sync", R_lpos, lpos_d)

        p.do("vector", lambda e: e.memset(cntm[:, :], 0.0), inc="v")
        p.do("vector", lambda e: e.memset(prefc[:, :], 0.0), inc="v")
        p.dma("sync", cntm[0:8, :], counts_d)
        p.dma("sync", prefc[0:8, :], pref_d)
        p.do("tensor", lambda e: e.matmul(ps[5][0:E2, 0:1], cntm[:, :], prefc[:, 0:1],
                                          start=True, stop=True), inc="t")
        p.do("vector", lambda e: e.memset(offc[:, :], 0.0), inc="v")
        p.do("scalar", lambda e: e.activation(out=offc[0:E2, 0:1], in_=ps[5][0:E2, 0:1],
                                              func=AF.Copy), inc="a")

        for qt in range(T // 512):
            p.do("tensor", lambda e, qt=qt: e.matmul(
                ps[4][:, 0:512], sb["onesk1"][0:1, :], R_route[0:1, qt * 512:(qt + 1) * 512],
                start=True, stop=True), inc="t")
            p.do("scalar", lambda e, qt=qt: e.activation(
                out=ohp_sb[:, qt * 512:(qt + 1) * 512], in_=ps[4][:, 0:512], func=AF.Copy), inc="a")
        p.do("vector", lambda e: e.tensor_scalar(ohp_sb[:, :], ohp_sb[:, :], iota128[:, 0:1], None,
                                                 op0=ALU.is_equal), inc="v")
        p.do("vector", lambda e: e.tensor_scalar(posk[:, :], ohp_sb[:, :], offc[:, 0:1], None,
                                                 op0=ALU.mult), inc="v")
        for qt in range(T // 512):
            p.do("tensor", lambda e, qt=qt: e.matmul(
                ps[4][0:1, 0:512], sb["ones128"][:, 0:1], posk[:, qt * 512:(qt + 1) * 512],
                start=True, stop=True), inc="t")
            p.do("vector", lambda e, qt=qt: e.tensor_tensor(
                R_ptok[0:1, qt * 512:(qt + 1) * 512], ps[4][0:1, 0:512],
                R_lpos[0:1, qt * 512:(qt + 1) * 512], op=ALU.add), inc="v")
        p.do("vector", lambda e: e.tensor_scalar(R_keep, R_ptok, float(CAP), None,
                                                 op0=ALU.is_lt), inc="v")
        for qt in range(T // 512):
            p.do("tensor", lambda e, qt=qt: e.matmul(
                ps[4][:, 0:512], sb["onesk1"][0:1, :], R_keep[0:1, qt * 512:(qt + 1) * 512],
                start=True, stop=True), inc="t")
            p.do("scalar", lambda e, qt=qt: e.activation(
                out=kbc[:, qt * 512:(qt + 1) * 512], in_=ps[4][:, 0:512], func=AF.Copy), inc="a")
        p.do("vector", lambda e: e.tensor_tensor(ohp_sb[:, :], ohp_sb[:, :], kbc[:, :],
                                                 op=ALU.mult), inc="v")
        p.do("vector", lambda e: e.tensor_tensor_scan(posk[:, :], ohp_sb[:, :], ohp_sb[:, :],
                                                      initial=0.0, op0=ALU.add, op1=ALU.bypass), inc="v")
        ckd = nc.dram_tensor("ckd", [E2, 1], F32)
        p.dma("sync", ckd[:, :], posk[0:E2, T - 1:T])
        p.dma("sync", R_r[0:1, 0:E2], ckd[:, 0:1])
        p.do("vector", lambda e: e.tensor_scalar(R_r[0:1, 8:8 + E2], R_r[0:1, 0:E2], 127.0, None,
                                                 op0=ALU.add), inc="v")
        p.do("scalar", lambda e: e.activation(out=icell[0:1, 0:E2], in_=R_r[0:1, 8:8 + E2],
                                              func=AF.Copy), inc="a")
        p.do("vector", lambda e: e.tensor_scalar(icell[0:1, 8:8 + E2], icell[0:1, 0:E2], 127, None,
                                                 op0=ALU.bitwise_and), inc="v")
        p.do("scalar", lambda e: e.activation(out=R_r[0:1, 16:16 + E2], in_=icell[0:1, 8:8 + E2],
                                              func=AF.Copy), inc="a")
        p.do("vector", lambda e: e.tensor_tensor(R_r[0:1, 8:8 + E2], R_r[0:1, 8:8 + E2],
                                                 R_r[0:1, 16:16 + E2], op=ALU.subtract), inc="v")
        p.do("vector", lambda e: e.tensor_tensor_scan(R_r[0:1, 24:24 + E2], R_r[0:1, 8:8 + E2],
                                                      R_r[0:1, 8:8 + E2], initial=0.0,
                                                      op0=ALU.add, op1=ALU.bypass), inc="v")
        p.do("vector", lambda e: e.tensor_tensor(R_r[0:1, 32:32 + E2], R_r[0:1, 24:24 + E2],
                                                 R_r[0:1, 8:8 + E2], op=ALU.subtract), inc="v")
        p.do("vector", lambda e: e.reduce_sum(R_r[0:1, 40:41], R_r[0:1, 8:8 + E2], axis=AX.X,
                                              opt_input=False, opt_output=False), inc="v")
        p.do("vector", lambda e: e.memset(gbc[:, :], 1e9), inc="v")
        p.do("tensor", lambda e: e.matmul(ps[5][0:E2, 0:1], R_r[0:1, 32:32 + E2],
                                          sb["ones128"][0:1, 0:1], start=True, stop=True), inc="t")
        p.do("scalar", lambda e: e.activation(out=gbc[0:E2, 0:1], in_=ps[5][0:E2, 0:1],
                                              func=AF.Copy), inc="a")
        p.do("vector", lambda e: e.tensor_scalar(posk[:, :], posk[:, :], gbc[:, 0:1], None,
                                                 op0=ALU.add), inc="v")
        p.do("vector", lambda e: e.tensor_tensor(posk[:, :], posk[:, :], ohp_sb[:, :],
                                                 op=ALU.mult), inc="v")
        for qt in range(T // 512):
            p.do("tensor", lambda e, qt=qt: e.matmul(
                ps[4][0:1, 0:512], sb["ones128"][:, 0:1], posk[:, qt * 512:(qt + 1) * 512],
                start=True, stop=True), inc="t")
            p.do("vector", lambda e, qt=qt: e.tensor_scalar(
                scr[0:1, 0:512], ps[4][0:1, 0:512], -1.0, None, op0=ALU.add), inc="v")
            p.do("vector", lambda e, qt=qt: e.tensor_scalar(
                kbc[0:1, qt * 512:(qt + 1) * 512],
                R_keep[0:1, qt * 512:(qt + 1) * 512], -float(NT * 128 + 1), float(NT * 128 + 1),
                op0=ALU.mult, op1=ALU.add), inc="v")
            p.do("vector", lambda e, qt=qt: e.tensor_tensor(
                scr[0:1, 0:512], scr[0:1, 0:512], kbc[0:1, qt * 512:(qt + 1) * 512],
                op=ALU.add), inc="v")
            for j4 in range(4):
                tt = qt * 4 + j4
                p.do("tensor", lambda e, j4=j4: e.matmul(
                    ps[5][:, 0:1], scr[0:1, j4 * 128:(j4 + 1) * 128], sb["ones128"][0:1, 0:1],
                    start=True, stop=True), inc="t")
                p.do("scalar", lambda e, tt=tt: e.activation(
                    out=desti[:, 2 * tt:2 * tt + 1], in_=ps[5][:, 0:1], func=AF.Copy), inc="a")
                p.do("tensor", lambda e, qt=qt, j4=j4: e.matmul(
                    ps[5][:, 1:2], R_keep[0:1, qt * 512 + j4 * 128:qt * 512 + (j4 + 1) * 128],
                    sb["ones128"][0:1, 0:1], start=True, stop=True), inc="t")
                p.do("scalar", lambda e, tt=tt: e.activation(
                    out=keepc[:, 2 * tt:2 * tt + 1], in_=ps[5][:, 1:2], func=AF.Copy), inc="a")
        p.do("tensor", lambda e: e.matmul(ps[4][:, 0:NT], sb["onesk1"][0:1, :], tbase[0:1, 0:NT],
                                          start=True, stop=True), inc="t")
        p.do("scalar", lambda e: e.activation(out=tbb[:, 0:NT], in_=ps[4][:, 0:NT],
                                              func=AF.Copy), inc="a")
        p.do("vector", lambda e: e.tensor_scalar(tbb[:, 0:NT], tbb[:, 0:NT], gbc[:, 0:1], None,
                                                 op0=ALU.is_ge), inc="v")
        p.do("tensor", lambda e: e.matmul(ps[4][0:1, 0:NT], sb["ones128"][:, 0:1], tbb[:, 0:NT],
                                          start=True, stop=True), inc="t")
        p.do("vector", lambda e: e.tensor_scalar(R_r[0:1, 48:48 + NT], ps[4][0:1, 0:NT], -1.0,
                                                 None, op0=ALU.add), inc="v")
        p.do("scalar", lambda e: e.activation(out=eidi[0:1, 0:NT], in_=R_r[0:1, 48:48 + NT],
                                              func=AF.Copy), inc="a")
        p.do("vector", lambda e: e.tensor_scalar(R_r[0:1, 48:48 + NT], tbase[0:1, 0:NT],
                                                 R_r[0:1, 40:41], None, op0=ALU.is_lt), inc="v")
        p.do("scalar", lambda e: e.activation(out=usei[0:1, 0:NT], in_=R_r[0:1, 48:48 + NT],
                                              func=AF.Copy), inc="a")

        p.dma("gpsimd", dbg_rows[0, :], R_ptok)
        p.dma("gpsimd", dbg_rows[1, :], R_keep)
        p.dma("gpsimd", dbg_rows[2, 0:128], R_r[0:1, 0:128])
        for tt in range(NTT):
            p.do("scalar", lambda e, tt=tt: e.activation(out=scr[:, 0:2], in_=desti[:, 2 * tt:2 * tt + 2], func=AF.Copy), inc="a")
            p.dma("gpsimd", dbg_rows[3, tt * 128:(tt + 1) * 128], scr[:, 0:1])
        if upto < 2:
            p.do("gpsimd", lambda e: e.nop(), inc="g")
            p.finalize(block)
            return nc
        p.do("vector", lambda e: e.memset(scr2[:, :], 0.0), inc="v")
        for j in range(NT + 1):
            p.dma("sync", sorted_x[j * 128:(j + 1) * 128, :].rearrange("(a p) d -> p (a d)", p=128),
                  scr2[:, 0:D])
            p.dma("sync", sorted_o[j * 128:(j + 1) * 128, :].rearrange("(a p) d -> p (a d)", p=128),
                  scr2[:, 0:D])
        for tt in range(NTT):
            p.dma("sync", scr2[:, 0:D],
                  xnt[tt * 128:(tt + 1) * 128, :].rearrange("(a p) d -> p (a d)", p=128))
            p.do("gpsimd", lambda e, tt=tt: e.indirect_dma_start(
                out=sorted_x[:, :], out_offset=IndirectOffsetOnAxis(ap=desti[:, 2 * tt:2 * tt + 1], axis=0),
                in_=scr2[:, 0:D], in_offset=None, oob_is_err=False), inc="d_gp", inc_step=16)

        p.dma("gpsimd", dbg_sx[:, :], sorted_x[:, :])
        if upto < 3:
            p.do("gpsimd", lambda e: e.nop(), inc="g")
            p.finalize(block)
            return nc
        id128s = sbuf("id128s", [128, 128], BF16)
        idd = nc.dram_tensor("c_id128w", [128, 128], BF16, kind="ExternalInput").ap()
        p.dma("sync", id128s[:, :], idd)
        for j in range(NT):
            p.dma("sync", scr2[:, 0:D],
                  sorted_x[j * 128:(j + 1) * 128, :].rearrange("(a p) d -> p (a d)", p=128))
            for ci in range(NC):
                p.do("tensor", lambda e, ci=ci: e.transpose(
                    psb[:, 0:128], scr2[:, ci * 128:(ci + 1) * 128], id128s[:, :]), inc="t")
                p.do("scalar", lambda e, j=j, ci=ci: e.activation(
                    out=xe_all[:, j * D + ci * 128:j * D + (ci + 1) * 128],
                    in_=psb[:, 0:128], func=AF.Copy), inc="a")

        if upto < 4:
            p.do("gpsimd", lambda e: e.nop(), inc="g")
            p.finalize(block)
            return nc
        for ex in range(E2):
            for hb in range(FF2 // HBW):
                for ci in range(NC):
                    p.dma("gpsimd", w1blk[:, ci * HBW:(ci + 1) * HBW],
                          w1_d[ex, ci * 128:(ci + 1) * 128, hb * HBW:(hb + 1) * HBW])
                for hq in range(NHQ):
                    p.dma("gpsimd", w2blk[:, hq * D:(hq + 1) * D],
                          w2_d[ex, hb * HBW + hq * 128:hb * HBW + (hq + 1) * 128, :])
                for j in range(NT):
                    def gmm(e, ex=ex, hb=hb, j=j):
                        with e.register(f"rg{uid()}") as r:
                            e.load(r, eidi[0:1, j:j + 1])
                            with e.If_eq(r, ex):
                                for hc in range(NHQ):
                                    for ci in range(NC):
                                        e.matmul(ps[hc // 2][:, (hc % 2) * 128:(hc % 2) * 128 + 128] if False else ps[hc % 4][:, (hc // 4) * 128:(hc // 4) * 128 + 128],
                                                 w1blk[:, ci * HBW + hc * 128:ci * HBW + (hc + 1) * 128],
                                                 xe_all[:, j * D + ci * 128:j * D + (ci + 1) * 128],
                                                 start=(ci == 0), stop=(ci == NC - 1))
                        return e.matmul(ps[6][0:1, 0:1], sb["ones128"][0:1, 0:1],
                                        sb["ones128"][0:1, 0:1], start=True, stop=True)
                    p.do("tensor", gmm, inc="t")
                    if upto == 31:
                        continue
                    def gev(e, ex=ex, hb=hb, j=j):
                        with e.register(f"ra{uid()}") as r:
                            e.load(r, eidi[0:1, j:j + 1])
                            with e.If_eq(r, ex):
                                for hc in range(NHQ):
                                    e.activation(
                                        out=htile[:, hc * 128:(hc + 1) * 128],
                                        in_=ps[hc % 4][:, (hc // 4) * 128:(hc // 4) * 128 + 128],
                                        func=AF.Gelu,
                                        bias=b1s[:, ex * NFC + hb * NHQ + hc:ex * NFC + hb * NHQ + hc + 1])
                        return e.activation(out=scr[0:2, 0:2], in_=scr[0:2, 0:2], func=AF.Copy)
                    p.do("scalar", gev, inc="a")
                    if upto == 32:
                        continue
                    def dmm(e, ex=ex, hb=hb, j=j):
                        with e.register(f"rd{uid()}") as r:
                            e.load(r, eidi[0:1, j:j + 1])
                            with e.If_eq(r, ex):
                                for db in range(NDB):
                                    for hc in range(NHQ):
                                        e.matmul(ps[4 + db][:, 0:DBW],
                                                 htile[:, hc * 128:(hc + 1) * 128],
                                                 w2blk[:, hc * D + db * DBW:hc * D + (db + 1) * DBW],
                                                 start=(hc == 0), stop=(hc == NHQ - 1))
                        return e.matmul(ps[6][0:1, 1:2], sb["ones128"][0:1, 0:1],
                                        sb["ones128"][0:1, 0:1], start=True, stop=True)
                    p.do("tensor", dmm, inc="t")
                    if upto == 33:
                        continue

                    def dev_(e, ex=ex, j=j):
                        with e.register(f"re{uid()}") as r:
                            e.load(r, eidi[0:1, j:j + 1])
                            with e.If_eq(r, ex):
                                for db in range(NDB):
                                    e.activation(out=ytmp[:, db * DBW:(db + 1) * DBW],
                                                 in_=ps[4 + db][:, 0:DBW], func=AF.Copy)
                        return e.activation(out=scr[0:1, 24:32], in_=scr[0:1, 24:32], func=AF.Copy)
                    p.do("scalar", dev_, inc="a")
                    if upto == 39:
                        continue
                    def yacc(e, ex=ex, hb=hb, j=j):
                        with e.register(f"ry{uid()}") as r:
                            e.load(r, eidi[0:1, j:j + 1])
                            with e.If_eq(r, ex):
                                for db in range(NDB):
                                    dst = y_all[:, j * D + db * DBW:j * D + (db + 1) * DBW]
                                    if hb == 0:
                                        e.tensor_copy(dst, ytmp[:, db * DBW:(db + 1) * DBW])
                                    else:
                                        e.tensor_tensor(dst, dst, ytmp[:, db * DBW:(db + 1) * DBW],
                                                        op=ALU.add)
                        return e.memset(scr[0:1, 0:8], 0.0)
                    p.do("gpsimd", yacc, inc="g")
            if 30 < upto <= 36:
                continue
            for dbh in range(NDB):
                p.dma("sync", scr[0:1, 0:DBW], b2_d[0:1, ex * D + dbh * DBW:ex * D + (dbh + 1) * DBW])
                p.do("tensor", lambda e: e.matmul(ps[6][:, 0:DBW], sb["onesk1"][0:1, :],
                                                  scr[0:1, 0:DBW], start=True, stop=True), inc="t")
                p.do("scalar", lambda e, dbh=dbh: e.activation(
                    out=b2bc[:, dbh * DBW:(dbh + 1) * DBW], in_=ps[6][:, 0:DBW], func=AF.Copy), inc="a")
            for j in range(NT):
                def outev(e, ex=ex, j=j):
                    with e.register(f"ro{uid()}") as r:
                        e.load(r, eidi[0:1, j:j + 1])
                        with e.If_eq(r, ex):
                            for db in range(NDB):
                                e.tensor_tensor(scr2[:, db * DBW:(db + 1) * DBW],
                                                y_all[:, j * D + db * DBW:j * D + (db + 1) * DBW],
                                                b2bc[:, db * DBW:(db + 1) * DBW], op=ALU.add)
                    return e.memset(scr[0:1, 8:16], 0.0)
                p.do("gpsimd", outev, inc="g")
                def odma(e, ex=ex, j=j):
                    with e.register(f"rm{uid()}") as r:
                        e.load(r, eidi[0:1, j:j + 1])
                        with e.If_eq(r, ex):
                            e.dma_start(out=sorted_o[j * 128:(j + 1) * 128, :]
                                        .rearrange("(a p) d -> p (a d)", p=128),
                                        in_=scr2[:, 0:D]).then_inc(p.sems["d_gp"], 16)
                        with e.Else():
                            e.dma_start(out=sorted_o[NT * 128:NT * 128 + 1, 0:2]
                                        .rearrange("a d -> a d"),
                                        in_=scr2[0:1, 0:2]).then_inc(p.sems["d_gp"], 16)
                    return e.nop()
                p.do("gpsimd", odma)
                p.bump("d_gp", 16)

        if (upto < 5 or upto > 30) and upto not in (41,):
            p.do("gpsimd", lambda e: e.nop(), inc="g")
            p.finalize(block)
            return nc
        grecv = sbuf("grecv", [128, 1024], BF16)
        for tt in range(NTT):
            p.dma("sync", scr2[:, 0:D],
                  xnt[tt * 128:(tt + 1) * 128, :].rearrange("(a p) d -> p (a d)", p=128))
            p.do("gpsimd", lambda e, tt=tt: e.indirect_dma_start(
                out=grecv[:, 0:D], out_offset=None,
                in_=sorted_o[:, :], in_offset=IndirectOffsetOnAxis(ap=desti[:, 2 * tt:2 * tt + 1], axis=0),
                oob_is_err=False), inc="d_gp", inc_step=16)
            p.do("vector", lambda e: e.tensor_tensor(scr[:, 0:D] if D <= 512 else scr[:, 0:512],
                                                     grecv[:, 0:min(D, 512)], scr2[:, 0:min(D, 512)],
                                                     op=ALU.subtract), inc="v")
            p.do("vector", lambda e, tt=tt: e.scalar_tensor_tensor(
                scr2[:, 0:min(D, 512)], scr[:, 0:min(D, 512)], keepc[:, 2 * tt:2 * tt + 1],
                scr2[:, 0:min(D, 512)], op0=ALU.mult, op1=ALU.add), inc="v")
            if D > 512:
                p.do("vector", lambda e: e.tensor_tensor(scr[:, 0:512],
                                                         grecv[:, 512:D], scr2[:, 512:D],
                                                         op=ALU.subtract), inc="v")
                p.do("vector", lambda e, tt=tt: e.scalar_tensor_tensor(
                    scr2[:, 512:D], scr[:, 0:512], keepc[:, 2 * tt:2 * tt + 1],
                    scr2[:, 512:D], op0=ALU.mult, op1=ALU.add), inc="v")
            for ci in range(NC):
                p.do("tensor", lambda e, ci=ci: e.transpose(
                    psb[:, 0:128], scr2[:, ci * 128:(ci + 1) * 128], id128s[:, :]), inc="t")
                p.do("scalar", lambda e, ci=ci: e.activation(
                    out=gfm[:, ci * 128:(ci + 1) * 128], in_=psb[:, 0:128], func=AF.Copy), inc="a")
            if upto == 41:
                continue
            for ci in range(NC):
                p.dma("sync", xres[:, 0:128], x2[ci * 128:(ci + 1) * 128, tt * 128:(tt + 1) * 128])
                p.dma("sync", scr2[:, 0:128].bitcast(BF16)[:, 0:128] if False else scr[:, 256:320].bitcast(BF16)[:, 0:128],
                      s2[ci, :, tt * 128:(tt + 1) * 128])
                p.do("scalar", lambda e, ci=ci: e.activation(
                    out=scr[:, 0:128], in_=gfm[:, ci * 128:(ci + 1) * 128], func=AF.Silu), inc="a")
                p.do("vector", lambda e: e.tensor_tensor(
                    scr[:, 128:256], scr[:, 256:320].bitcast(BF16)[:, 0:128],
                    scr[:, 0:128], op=ALU.mult), inc="v")
                p.do("vector", lambda e: e.tensor_tensor(
                    xres[:, 0:128], xres[:, 0:128], scr[:, 128:256], op=ALU.add), inc="v")
                p.dma("scalar", xout[ci * 128:(ci + 1) * 128, tt * 128:(tt + 1) * 128],
                      xres[:, 0:128])

        p.do("gpsimd", lambda e: e.nop(), inc="g")
        p.finalize(block)
    return nc


def run_l2(x_full, Wq, Wk, Wv, Wout, W_sw, b_sw, W1, b1, W2, b2, cfg=None):
    cfg = cfg or CFG
    B, S, D = x_full.shape
    T = S // 2
    NC = D // 128
    E2, FF2 = cfg["E2"], cfg["FF2"]
    NFC = FF2 // 128
    consts = host_constants(cfg)
    wqkv = np.concatenate([Wq.reshape(D, D), Wk.reshape(D, D), Wv.reshape(D, D)], axis=1)
    woutf = Wout.reshape(D, D)
    common_a = dict(
        wqkv=np.ascontiguousarray(wqkv.astype(np.float32)),
        wout=np.ascontiguousarray(woutf.astype(np.float32)),
        wsw=np.ascontiguousarray(W_sw.astype(np.float32)),
        bsw=np.asarray(b_sw, np.float32).reshape(1, E2),
        dmask_w=consts["dmask"], c_iota128=consts["iota128"], c_iota8bc=consts["iota8bc"],
        c_ones128=consts["ones128"], c_onesk1=consts["onesk1"],
        c_elow=consts["elow"], c_ehigh=consts["ehigh"], c_ones64x2=consts["ones64x2"],
    )
    ra = {}
    for h in (0, 1):
        nc = build_l2a(cfg, h)
        cores = [2 * b + h for b in range(B)]
        in_maps = [dict(xb=np.ascontiguousarray(x_full[b].T.astype(np.float32)), **common_a)
                   for b in range(B)]
        res = run_bass_kernel_spmd(nc, in_maps, core_ids=cores)
        for i, b in enumerate(range(B)):
            ra[(b, h)] = res.results[i]
    counts_all = np.zeros((8, E2), np.float32)
    for b in range(B):
        for h in (0, 1):
            counts_all[2 * b + h] = ra[(b, h)]["counts_out"][:, 0]
    b1h = np.zeros((128, E2 * NFC), np.float32)
    for e in range(E2):
        for hcc in range(NFC):
            b1h[:, e * NFC + hcc] = b1[e, hcc * 128:(hcc + 1) * 128]
    b2h = np.ascontiguousarray(b2.reshape(1, E2 * D).astype(np.float32))
    nc2 = build_l2b(cfg)
    in_maps = []
    cores = []
    for b in range(B):
        for h in (0, 1):
            c = 2 * b + h
            pref = np.zeros((8, 2), np.float32)
            pref[:c, 0] = 1.0
            r = ra[(b, h)]
            in_maps.append(dict(
                x2=np.ascontiguousarray(x_full[b].T[:, h * T:(h + 1) * T].astype(np.float32)),
                s2=r["s_out"], xnt=r["xnt_out"],
                routes_d=r["routes_out"], lpos_d=r["lpos_out"],
                counts_d=counts_all, pref_d=pref,
                m2_w1=np.ascontiguousarray(W1.astype(np.float32)),
                m2_w2=np.ascontiguousarray(W2.astype(np.float32)),
                m2_b1=b1h, m2_b2=b2h, c_id128w=consts["id128"],
                c_iota128=consts["iota128"], c_tbase=consts["tbase"],
                c_ones128=consts["ones128"], c_onesk1=consts["onesk1"]))
            cores.append(c)
    res = run_bass_kernel_spmd(nc2, in_maps, core_ids=cores)
    out = np.zeros((B, S, D), np.float32)
    for i, c in enumerate(cores):
        b, h = c // 2, c % 2
        out[b, h * T:(h + 1) * T, :] = res.results[i]["xout"].T
    return out


def kernel(**inputs):
    cfg = CFG
    x = np.asarray(inputs["L0_attn_x"], np.float32)
    x1 = run_l0(x, np.asarray(inputs["L0_attn_W_qkv"]),
                np.asarray(inputs["L0_ffn_w_gate"]), np.asarray(inputs["L0_ffn_w_up"]),
                np.asarray(inputs["L0_ffn_w_down"]), cfg)
    x2 = run_l1(x1, np.asarray(inputs["L1_attn_w_qkv"]), np.asarray(inputs["L1_attn_w_out"]),
                np.asarray(inputs["L1_ffn_ln_gamma"]), np.asarray(inputs["L1_ffn_ln_beta"]),
                np.asarray(inputs["L1_ffn_mu"]), np.asarray(inputs["L1_ffn_scale"]).reshape(()),
                np.asarray(inputs["L1_ffn_w1"]), np.asarray(inputs["L1_ffn_b1"]),
                np.asarray(inputs["L1_ffn_w2"]), np.asarray(inputs["L1_ffn_b2"]), cfg)
    x3 = run_l2(x2, np.asarray(inputs["L2_attn_Wq"]), np.asarray(inputs["L2_attn_Wk"]),
                np.asarray(inputs["L2_attn_Wv"]), np.asarray(inputs["L2_attn_Wout"]),
                np.asarray(inputs["L2_ffn_W_sw"]), np.asarray(inputs["L2_ffn_b_sw"]),
                np.asarray(inputs["L2_ffn_W1"]), np.asarray(inputs["L2_ffn_b1"]),
                np.asarray(inputs["L2_ffn_W2"]), np.asarray(inputs["L2_ffn_b2"]), cfg)
    x4 = run_l3(x3, np.asarray(inputs["L3_attn_Wq"]), np.asarray(inputs["L3_attn_Wk"]),
                np.asarray(inputs["L3_attn_Wv"]), np.asarray(inputs["L3_attn_Wo"]),
                np.asarray(inputs["L3_ffn_w_gate"]), np.asarray(inputs["L3_ffn_w_up"]),
                np.asarray(inputs["L3_ffn_w_down"]), cfg)
    return x4
